# version 05
# baseline (speedup 1.0000x reference)
import contextlib
import math
import os

KLIM = int(os.environ.get("KLIM", "99"))

import numpy as np
import ml_dtypes

import concourse.bass as bass
import concourse.mybir as mybir
from concourse.bass_utils import run_bass_kernel_spmd

F32 = mybir.dt.float32
BF16 = mybir.dt.bfloat16
FP16 = mybir.dt.float16
FP8 = mybir.dt.float8e4
ALU = mybir.AluOpType
AF = mybir.ActivationFunctionType
AX = mybir.AxisListType
RED = None
BF = ml_dtypes.bfloat16

B, T, D = 4, 2048, 2048
H, HKV, HD = 32, 8, 64
FF = 8192
NB, OB, OT = 16, 8, 1024
DC, FC = 16, 64
EXPB = -20.0

ENGINES = ("ten", "vec", "sca", "gps", "syn")
DMA_QUEUES = ("syn", "gps", "sca")
N_SLOTS = 6


def ilo(kb):
    return kb // 2



def _ff_map():
    m = np.arange(FF)
    return (m % 128) * 64 + (m // 128)


def _hadamard128():
    i = np.arange(128)
    a = i[:, None] & i[None, :]
    pc = np.zeros((128, 128), dtype=np.int64)
    for b in range(8):
        pc += (a >> b) & 1
    return np.where(pc % 2 == 0, 1.0, -1.0).astype(BF)


_ROPE_CACHE = {}


def _rope_full_tables():
    if "cs" not in _ROPE_CACHE:
        import jax
        import jax.numpy as jnp
        with jax.default_device(jax.devices("cpu")[0]):
            theta = 1.0 / 10000.0 ** (jnp.arange(0, HD, 2, dtype=jnp.float32) / HD)
            t = jnp.arange(T, dtype=jnp.float32)
            freqs = t[:, None] * theta[None, :]
            cos = np.asarray(jax.device_get(jnp.cos(freqs)))
            sin = np.asarray(jax.device_get(jnp.sin(freqs)))
        _ROPE_CACHE["cs"] = (cos.T.astype(np.float32), sin.T.astype(np.float32))
    return _ROPE_CACHE["cs"]


def _rope_tables(positions):
    cos_all, sin_all = _rope_full_tables()
    pos = np.asarray(positions, dtype=np.int64)
    cos, sin = cos_all[:, pos], sin_all[:, pos]
    C = np.zeros((128, len(positions)), dtype=np.float64)
    S = np.zeros((128, len(positions)), dtype=np.float64)
    for half in range(2):
        r = half * 64
        C[r + 0:r + 64:2] = cos
        C[r + 1:r + 64:2] = cos
        S[r + 0:r + 64:2] = -sin
        S[r + 1:r + 64:2] = sin
    return C.astype(np.float32), S.astype(np.float32)


def _p2_swap():
    P = np.zeros((128, 128), dtype=np.float32)
    for half in range(2):
        r = half * 64
        for j in range(32):
            P[r + 2 * j, r + 2 * j + 1] = 1.0
            P[r + 2 * j + 1, r + 2 * j] = 1.0
    return P.astype(BF)


def host_prep(inputs):
    x = np.asarray(inputs["L0_attn_x"], dtype=np.float32)
    wq = np.asarray(inputs["L0_attn_wq"], dtype=np.float32)
    wk = np.asarray(inputs["L0_attn_wk"], dtype=np.float32)
    wv = np.asarray(inputs["L0_attn_wv"], dtype=np.float32)
    wo = np.asarray(inputs["L0_attn_wo"], dtype=np.float32)
    wg = np.asarray(inputs["L0_ffn_w_gate"], dtype=np.float32)
    wu = np.asarray(inputs["L0_ffn_w_up"], dtype=np.float32)
    wd = np.asarray(inputs["L0_ffn_w_down"], dtype=np.float32)

    d2 = _ff_map()
    sh = {}
    evens = [h for h in range(H) if (h // 4) % 2 == 0]
    odds = [h for h in range(H) if (h // 4) % 2 == 1]
    qcols = []
    for c in range(H // 2):
        qcols += list(range(evens[c] * HD, evens[c] * HD + HD))
        qcols += list(range(odds[c] * HD, odds[c] * HD + HD))
    qcols = np.array(qcols)
    sh["wqT"] = np.ascontiguousarray(wq.T[:, qcols])
    sh["wkT"] = np.ascontiguousarray(wk.T)
    sh["wvT"] = np.ascontiguousarray(wv.T)
    sh["woT"] = np.ascontiguousarray(wo.T[qcols, :])
    for name, w in (("wgS", wg), ("wuS", wu)):
        wt = w.T[:, d2]
        sh[name] = np.ascontiguousarray(
            wt.reshape(DC, 128, FC, 128).transpose(2, 0, 1, 3))
    sh["wdS"] = np.ascontiguousarray(wd.T[d2, :].reshape(FC, 128, D))
    sh["ident"] = np.eye(128, dtype=BF)
    sh["ident32"] = np.eye(128, dtype=np.float32)
    sh["onesf"] = np.ones((1, 128), dtype=np.float32)
    sh["h128"] = _hadamard128()
    sh["p2"] = _p2_swap()

    cores = []
    for core in range(8):
        s, par = core // 2, core % 2
        perm = np.arange(NB)
        if par == 1:
            perm = perm.reshape(-1, 2)[:, ::-1].ravel()
        tok_perm = (perm[:, None] * 128 + np.arange(128)[None, :]).ravel()
        own_tok = tok_perm.reshape(NB, 128)[0::2].ravel()
        d = {}
        d["x_seq"] = np.ascontiguousarray(x[s][tok_perm])
        d["c_q"], d["s_q"] = _rope_tables(own_tok)
        d["c_kp"], d["s_kp"] = _rope_tables(tok_perm)
        mk = np.ones((NB, 128, 128), dtype=np.float32)
        for kb in range(NB):
            i = ilo(kb)
            if kb == 2 * i:
                mk[kb] = np.tril(np.ones((128, 128), np.float32)).T
            elif par == 0:
                mk[kb] = 0.0
        d["maskT"] = mk.astype(BF)
        d["own_tok"] = own_tok
        cores.append(d)
    return sh, cores



class PB:
    def __init__(self, nc):
        self.nc = nc
        self.ops = {e: [] for e in ENGINES}
        self.cnt = {e: 0 for e in ENGINES}
        self.observed = {e: {} for e in ENGINES}
        self.writer = {}
        self.readers = {}
        self.slot_use = {q: [0] * N_SLOTS for q in DMA_QUEUES}
        self.slot_rr = {q: 0 for q in DMA_QUEUES}
        self.sems = {}
        self.out_tokens = []

    def alloc_sems(self, ctx):
        e = ctx.enter_context
        for en in ENGINES:
            self.sems["e_" + en] = e(self.nc.semaphore("e_" + en))
        for q in DMA_QUEUES:
            for s in range(N_SLOTS):
                self.sems[f"d_{q}_{s}"] = e(self.nc.semaphore(f"d_{q}_{s}"))


    def _filter(self, eng, toks):
        best = {}
        for sem, val in toks:
            if val > best.get(sem, 0):
                best[sem] = val
        waits = []
        for sem, val in best.items():
            if self.observed[eng].get(sem, 0) < val:
                waits.append((sem, val))
                self.observed[eng][sem] = val
        return waits

    def _deps(self, reads, writes):
        toks = []
        for b in reads:
            w = self.writer.get(b)
            if w is not None:
                toks.append(w)
        for b in writes:
            w = self.writer.get(b)
            if w is not None:
                toks.append(w)
            toks.extend(self.readers.get(b, ()))
        return toks

    def _update(self, tok, reads, writes):
        for b in writes:
            self.writer[b] = tok
            self.readers[b] = []
        for b in reads:
            if b not in writes:
                self.readers.setdefault(b, []).append(tok)

    def op(self, eng, fn, reads=(), writes=()):
        waits = self._filter(eng, self._deps(reads, writes))
        self.cnt[eng] += 1
        tok = ("e_" + eng, self.cnt[eng])
        self.ops[eng].append((waits, fn, tok))
        self._update(tok, reads, writes)
        return tok

    def dma(self, q, out_ap, in_ap, reads=(), writes=(), is_output=False):
        s = self.slot_rr[q]
        self.slot_rr[q] = (s + 1) % N_SLOTS
        sem = f"d_{q}_{s}"
        uses = self.slot_use[q][s]
        toks = ([(sem, 16 * uses)] if uses else []) + self._deps(reads, writes)
        waits = self._filter(q, toks)
        self.slot_use[q][s] = uses + 1
        tok = (sem, 16 * (uses + 1))

        def fn(eng, out_ap=out_ap, in_ap=in_ap):
            return eng.dma_start(out=out_ap, in_=in_ap)

        self.ops[q].append((waits, fn, tok))
        self._update(tok, reads, writes)
        if is_output:
            self.out_tokens.append(tok)
        return tok

    def barrier(self):
        toks = [("e_" + e2, self.cnt[e2]) for e2 in ENGINES if self.cnt[e2] > 0]
        for q in DMA_QUEUES:
            for s in range(N_SLOTS):
                if self.slot_use[q][s] > 0:
                    toks.append((f"d_{q}_{s}", 16 * self.slot_use[q][s]))
        for e in ENGINES:
            waits = self._filter(e, toks)
            if waits:
                self.ops[e].append((waits, None, None))

    def emit(self, block):
        sems = self.sems
        emitted = {}
        fin = {}
        for sem, val in self.out_tokens:
            q = sem.split("_")[1]
            cur = fin.setdefault(q, {})
            cur[sem] = max(cur.get(sem, 0), val)

        def mk(name):
            def body(eng):
                for waits, fn, tok in self.ops[name]:
                    for wsem, wval in waits:
                        eng.wait_ge(sems[wsem], wval)
                    if fn is None:
                        continue
                    sem, val = tok
                    ins = fn(eng)
                    ins.then_inc(sems[sem], val - emitted.get(sem, 0))
                    emitted[sem] = val
                for sem, val in fin.get(name, {}).items():
                    eng.wait_ge(sems[sem], val)
            return body

        block.tensor(mk("ten"))
        block.vector(mk("vec"))
        block.scalar(mk("sca"))
        block.gpsimd(mk("gps"))
        block.sync(mk("syn"))


XLT = 0
KRO = 32768
VAU = 40960
QRO = 49280
YFM = 65664
PRB = 0
WOT = 0
AFM = 32768
ATM = 49280
X1S = 51328
QXS = 53376
S1O = 0
QXT = 65664
DFM = 65664
RS3 = 82048
BIGN = 83072

PXA = 0
PXB = 2048
PCQ = 0
PSQ = 1024
PXO = 0
PT0 = 2048
PSB = 1024
PG = 3072
PGP = 1024
PN = 4096

STAT = 56
SA, SB = 0, 8
MG, MU, MD = 16, 17, 18
MGH, MUH = 19, 20
MGN, MUN = 21, 22
MDH, MDN = 23, 24
MACC = 25
MPART = 28
MUMD = 29
DQ3 = 32
GCL = 40
LN4 = 48
EXB = 55


def build_nc():
    nc = bass.Bass("TRN2", target_bir_lowering=False)
    g = {}

    def din(name, shape, dt=F32):
        g[name] = nc.dram_tensor(name, shape, dt, kind="ExternalInput").ap()

    din("x_seq", [T, D])
    din("wqT", [D, D])
    din("wkT", [D, HKV * HD])
    din("wvT", [D, HKV * HD])
    din("woT", [D, D])
    din("wgS", [FC, DC, 128, 128])
    din("wuS", [FC, DC, 128, 128])
    din("wdS", [FC, 128, D])
    din("ident", [128, 128], BF16)
    din("ident32", [128, 128])
    din("onesf", [1, 128])
    din("h128", [128, 128], BF16)
    din("p2", [128, 128], BF16)
    din("c_kp", [128, T])
    din("s_kp", [128, T])
    din("c_q", [128, OT])
    din("s_q", [128, OT])
    din("maskT", [NB, 128, 128], BF16)

    g["out"] = nc.dram_tensor("out", [OT, D], F32, kind="ExternalOutput").ap()
    import os as _os
    _dbg = _os.environ.get("KDBG", "") == "1"
    if _os.environ.get("KDBG2", "") == "1":
        g["dbg_qxt"] = nc.dram_tensor("dbg_qxt", [128, 16384], F32, kind="ExternalOutput").ap()
        g["dbg_s1"] = nc.dram_tensor("dbg_s1", [128, 65536], F32, kind="ExternalOutput").ap()
        g["dbg_dfm"] = nc.dram_tensor("dbg_dfm", [128, 16384], F32, kind="ExternalOutput").ap()
    g["x1_park"] = nc.dram_tensor("x1_park", [OT, D], BF16,
                                  kind="ExternalOutput" if _dbg else "Internal").ap()
    g["td_park"] = nc.dram_tensor("td_park", [FC, 128, D], FP8, kind="Internal").ap()

    with contextlib.ExitStack() as ctx:
        e = ctx.enter_context
        bufs = {}
        bufs["big"] = e(nc.sbuf_tensor("big", [128, BIGN], BF16))
        bufs["pool"] = e(nc.sbuf_tensor("pool", [128, PN], F32))
        bufs["u16"] = e(nc.sbuf_tensor("u16", [128, 1024], FP16))
        bufs["c16"] = e(nc.sbuf_tensor("c16", [128, 1024], FP16))
        bufs["wraw"] = e(nc.sbuf_tensor("wraw", [128, 2048], F32))
        bufs["tq"] = e(nc.sbuf_tensor("tq", [128, 2048], BF16))
        bufs["c_id"] = e(nc.sbuf_tensor("c_id", [128, 128], BF16))
        bufs["c_idf"] = e(nc.sbuf_tensor("c_idf", [128, 128], F32))
        bufs["c_of"] = e(nc.sbuf_tensor("c_of", [1, 128], F32))
        bufs["c_h"] = e(nc.sbuf_tensor("c_h", [128, 128], BF16))
        bufs["c_p2"] = e(nc.sbuf_tensor("c_p2", [128, 128], BF16))
        bufs["stat"] = e(nc.sbuf_tensor("stat", [128, STAT], F32))
        bufs["ps_a"] = e(nc.psum_tensor("ps_a", [128, 1024], F32))
        bufs["ps_b"] = e(nc.psum_tensor("ps_b", [128, 1024], F32))
        bufs["ps_c"] = e(nc.psum_tensor("ps_c", [128, 512], F32))
        bufs["ps_d"] = e(nc.psum_tensor("ps_d", [128, 512], F32))
        bufs["pb0"] = e(nc.psum_tensor("pb0", [128, 1024], BF16))
        bufs["pb1"] = e(nc.psum_tensor("pb1", [128, 1024], BF16))

        p = PB(nc)
        p.alloc_sems(ctx)
        with nc.allow_low_precision(reason="bf16 operands with fp32 psum accumulation; tolerance-sanctioned"):
            build_body(nc, p, g, bufs)
            blk = e(nc.Block())
            p.emit(blk)
    return nc


def build_body(nc, p, g, bufs):
    import concourse.bass_isa as bass_isa
    big = bufs["big"]; pool = bufs["pool"]; u16 = bufs["u16"]; c16 = bufs["c16"]
    wraw = bufs["wraw"]; tq = bufs["tq"]
    c_id = bufs["c_id"]; c_idf = bufs["c_idf"]; c_of = bufs["c_of"]
    c_h = bufs["c_h"]; c_p2 = bufs["c_p2"]
    stat = bufs["stat"]
    srow = bufs["pool"][0:1, 1024:2048]
    c_or = bufs["c_h"][0:1, :]
    c_oc = bufs["c_h"][:, 0:1]
    c_msk = bufs["big"][:, 4096:4096 + NB * 128]
    rrow = bufs["big"][0:1, 2048:2560]
    ps_a = bufs["ps_a"]; ps_b = bufs["ps_b"]; ps_c = bufs["ps_c"]
    ps_d = bufs["ps_d"]; pb0 = bufs["pb0"]; pb1 = bufs["pb1"]
    RMAX = bass_isa.ReduceOp.max

    def bg(off, n):
        return big[:, off:off + n]

    def pl(off, n):
        return pool[:, off:off + n]

    def st(c):
        return stat[:, c:c + 1]

    p.dma("syn", c_id[:], g["ident"], writes=["c_id"])
    p.dma("syn", c_idf[:], g["ident32"], writes=["c_idf"])
    p.dma("syn", c_of[:], g["onesf"], writes=["c_of"])
    p.op("vec", lambda v: v.memset(st(EXB), EXPB), writes=["exb"])
    p.dma("syn", c_h[:], g["h128"], writes=["c_h"])
    p.dma("syn", c_p2[:], g["p2"], writes=["c_p2"])


    if KLIM < 0:
        return
    def ln_stats(tile_ap, n, sbase, key_r, junk_ap, junk_key):
        mu, ex2, t2, sr, rs = (st(sbase + i) for i in (0, 1, 2, 3, 4))
        kb = f"st{sbase}_"
        p.op("vec", lambda v: v.reduce_sum(mu, tile_ap, axis=AX.X),
             reads=[key_r], writes=[kb + "mu"])
        p.op("sca", lambda s: s.activation(junk_ap, tile_ap, AF.Square, accum_out=ex2),
             reads=[key_r], writes=[junk_key, kb + "ex"])
        inv = 1.0 / n
        p.op("vec", lambda v: v.tensor_scalar(mu, mu, inv, None, ALU.mult),
             reads=[kb + "mu"], writes=[kb + "mu"])
        p.op("vec", lambda v: v.tensor_scalar(ex2, ex2, inv, None, ALU.mult),
             reads=[kb + "ex"], writes=[kb + "ex"])
        p.op("vec", lambda v: v.tensor_tensor(t2, mu, mu, ALU.mult),
             reads=[kb + "mu"], writes=[kb + "t2"])
        p.op("vec", lambda v: v.tensor_tensor(ex2, ex2, t2, ALU.subtract),
             reads=[kb + "ex", kb + "t2"], writes=[kb + "ex"])
        p.op("vec", lambda v: v.tensor_scalar(ex2, ex2, 1e-5, None, ALU.add),
             reads=[kb + "ex"], writes=[kb + "ex"])
        p.op("sca", lambda s: s.activation(sr, ex2, AF.Sqrt),
             reads=[kb + "ex"], writes=[kb + "sr"])
        p.op("vec", lambda v: v.reciprocal(rs, sr),
             reads=[kb + "sr"], writes=[kb + "rs"])
        p.op("vec", lambda v: v.tensor_tensor(sr, rs, rs, ALU.mult),
             reads=[kb + "rs"], writes=[kb + "sr"])
        p.op("vec", lambda v: v.tensor_tensor(sr, sr, ex2, ALU.mult),
             reads=[kb + "sr", kb + "ex"], writes=[kb + "sr"])
        p.op("vec", lambda v: v.tensor_scalar(sr, sr, -0.5, 1.5, ALU.mult, ALU.add),
             reads=[kb + "sr"], writes=[kb + "sr"])
        p.op("vec", lambda v: v.tensor_tensor(rs, rs, sr, ALU.mult),
             reads=[kb + "rs", kb + "sr"], writes=[kb + "rs"])
        return mu, rs, kb + "mu", kb + "rs"

    def ladder(zq_ap, kzq, out_ap, kout, n):
        u = u16[:, 0:n]
        c = c16[:, 0:n]
        p.op("vec", lambda v: v.tensor_scalar(u, zq_ap, 128.5, 1.0 / 512, ALU.add, ALU.mult),
             reads=[kzq], writes=["lad_u"])
        for _ in range(9):
            p.op("vec", lambda v: v.tensor_scalar(c, u, 0.5, None, ALU.is_ge),
                 reads=["lad_u"], writes=["lad_c"])
            p.op("vec", lambda v: v.scalar_tensor_tensor(u, u, 2.0, c, ALU.mult, ALU.subtract),
                 reads=["lad_u", "lad_c"], writes=["lad_u"])
        p.op("vec", lambda v: v.scalar_tensor_tensor(out_ap, zq_ap, 0.5, u, ALU.add, ALU.subtract),
             reads=[kzq, "lad_u"], writes=[kout])
        p.op("vec", lambda v: v.tensor_scalar(out_ap, out_ap, -128.0, 127.0, ALU.max, ALU.min),
             reads=[kout], writes=[kout])

    xs_d = g["x_seq"].rearrange("(n p) d -> n p d", p=128)
    for i in range(NB):
        par = i % 2
        xa = pl(PXA if par == 0 else PXB, 2048)
        kx = f"poolx{par}"
        p.dma("syn", xa, xs_d[i], writes=[kx])
        xb = tq[:, 0:2048] if par == 0 else bg(YFM, 2048)
        kxb = f"xb{par}"
        mu, rs, kmu, krs = ln_stats(xa, D, SA if par == 0 else SB, kx, xb, kxb)
        p.op("vec", lambda v, xa=xa, xb=xb, mu=mu, rs=rs: v.tensor_scalar(
            xb, xa, mu, rs, ALU.subtract, ALU.mult),
            reads=[kx, kmu, krs], writes=[kxb])
        for dc in range(DC):
            pt = (pb0 if dc % 2 == 0 else pb1)[:, 0:128]
            kp = f"pb{dc % 2}"
            p.op("ten", lambda t, xb=xb, pt=pt, dc=dc: t.transpose(
                pt, xb[:, dc * 128:dc * 128 + 128], c_id[:]),
                reads=[kxb, "c_id"], writes=[kp])
            p.op("sca", lambda s, pt=pt, dst=bg(XLT + dc * 2048 + i * 128, 128):
                 s.copy(dst, pt), reads=[kp], writes=["xlt"])

    if KLIM < 1:
        return
    p.dma("syn", pl(PXA, 2048), g["c_kp"], writes=["poolx0"])
    p.dma("syn", pl(PXB, 2048), g["s_kp"], writes=["poolx1"])

    p.dma("gps", bg(QRO, 8192).rearrange("p (dc v) -> p dc v", v=512),
          g["wkT"].rearrange("(dc p) v -> p dc v", p=128), writes=["wkb"])
    p.dma("gps", bg(QRO + 8192, 8192).rearrange("p (dc v) -> p dc v", v=512),
          g["wvT"].rearrange("(dc p) v -> p dc v", p=128), writes=["wvb"])

    for i in range(NB):
        pm, kpm = (ps_c, "ps_c") if i % 2 == 0 else (ps_d, "ps_d")

        def vmm(t, i=i, pm=pm):
            r = None
            for dc in range(DC):
                r = t.matmul(pm[:, 0:512], bg(XLT + dc * 2048 + i * 128, 128),
                             bg(QRO + 8192 + dc * 512, 512),
                             start=(dc == 0), stop=(dc == DC - 1))
            return r
        p.op("ten", vmm, reads=["xlt", "wvb"], writes=[kpm])
        vdst = bg(VAU + i * 520, 520).rearrange("p (h v) -> p h v", v=65)
        p.op("vec", lambda v, pm=pm, vdst=vdst: v.tensor_copy(
            vdst[:, :, 0:64], pm[:, 0:512].rearrange("p (h v) -> p h v", v=64)),
            reads=[kpm], writes=["vau"])
        p.op("vec", lambda v, vdst=vdst: v.memset(vdst[:, :, 64:65], 1.0),
             writes=["vau"])

    for kc in range(4):
        for tc in range(4):
            pm, kpm = (ps_c, "ps_c") if tc % 2 == 0 else (ps_d, "ps_d")

            def kmm(t, kc=kc, tc=tc, pm=pm):
                r = None
                for dc in range(DC):
                    r = t.matmul(pm[:, 0:512], bg(QRO + dc * 512 + kc * 128, 128),
                                 bg(XLT + dc * 2048 + tc * 512, 512),
                                 start=(dc == 0), stop=(dc == DC - 1))
                return r
            p.op("ten", kmm, reads=["xlt", "wkb"], writes=[kpm])
            kraw = tq[:, 0:512] if tc % 2 == 0 else tq[:, 512:1024]
            kkr = f"tqk{tc % 2}"
            p.op("sca", lambda s, pm=pm, kraw=kraw: s.copy(kraw, pm[:, 0:512]),
                 reads=[kpm], writes=[kkr])
            p.op("ten", lambda t, kraw=kraw, pm=pm: t.matmul(
                pm[:, 0:512], c_p2[:], kraw, start=True, stop=True),
                reads=[kkr, "c_p2", kpm], writes=[kpm])
            t1 = u16[:, 0:512]
            t2 = c16[:, 0:512]
            p.op("vec", lambda v, kraw=kraw, tc=tc, t1=t1: v.tensor_tensor(
                t1, kraw, pl(PXA + tc * 512, 512), ALU.mult),
                reads=[kkr, "poolx0"], writes=["ropet1"])
            p.op("vec", lambda v, pm=pm, tc=tc, t2=t2: v.tensor_tensor(
                t2, pm[:, 0:512], pl(PXB + tc * 512, 512), ALU.mult),
                reads=[kpm, "poolx1"], writes=["ropet2"])
            p.op("vec", lambda v, kc=kc, tc=tc, t1=t1, t2=t2: v.tensor_tensor(
                bg(KRO + kc * 2048 + tc * 512, 512), t1, t2, ALU.add),
                reads=["ropet1", "ropet2"], writes=["kro"])

    if KLIM < 2:
        return
    p.barrier()

    KSUB = int(os.environ.get("KSUB", "99"))
    p.dma("syn", pl(PCQ, 1024), g["c_q"], reads=["poolx0"], writes=["poolcq"])
    p.dma("syn", pl(PSQ, 1024), g["s_q"], reads=["poolx0", "poolx1"], writes=["poolsq"])
    for og in range(4):
        wqg = bg(YFM, 8192)
        p.dma("gps", wqg.rearrange("p (dc v) -> p dc v", v=512),
              g["wqT"][:, og * 512:og * 512 + 512].rearrange(
                  "(dc p) v -> p dc v", p=128), writes=["wqg"])
        for oc4 in range(4):
            oc = og * 4 + oc4
            for tc in range(2):
                pm, kpm = (ps_c, "ps_c") if (oc4 * 2 + tc) % 2 == 0 else (ps_d, "ps_d")

                def qmm(t, oc4=oc4, tc=tc, pm=pm):
                    r = None
                    for dc in range(DC):
                        mv = bg(XLT + dc * 2048 + tc * 1024, 1024).rearrange(
                            "p (b t) -> p b t", t=128)[:, 0:8:2, :]
                        r = t.matmul(pm[:, 0:512],
                                     bg(YFM + dc * 512 + oc4 * 128, 128), mv,
                                     start=(dc == 0), stop=(dc == DC - 1))
                    return r
                if KSUB < 1:
                    continue
                p.op("ten", qmm, reads=["xlt", "wqg"], writes=[kpm])
                if KSUB < 2:
                    continue
                qraw = tq[:, 0:512] if tc == 0 else tq[:, 512:1024]
                kqr = f"tqk{tc}"
                p.op("sca", lambda s, pm=pm, qraw=qraw: s.copy(qraw, pm[:, 0:512]),
                     reads=[kpm], writes=[kqr])
                if KSUB < 3:
                    continue
                p.op("ten", lambda t, qraw=qraw, pm=pm: t.matmul(
                    pm[:, 0:512], c_p2[:], qraw, start=True, stop=True),
                    reads=[kqr, "c_p2", kpm], writes=[kpm])
                t1 = u16[:, 0:512]
                t2 = c16[:, 0:512]
                p.op("vec", lambda v, qraw=qraw, tc=tc, t1=t1: v.tensor_tensor(
                    t1, qraw, pl(PCQ + tc * 512, 512), ALU.mult),
                    reads=[kqr, "poolcq"], writes=["ropet1"])
                p.op("vec", lambda v, pm=pm, tc=tc, t2=t2: v.tensor_tensor(
                    t2, pm[:, 0:512], pl(PSQ + tc * 512, 512), ALU.mult),
                    reads=[kpm, "poolsq"], writes=["ropet2"])
                p.op("vec", lambda v, oc=oc, tc=tc, t1=t1, t2=t2: v.tensor_tensor(
                    bg(QRO + oc * 1024 + tc * 512, 512), t1, t2, ALU.add),
                    reads=["ropet1", "ropet2"], writes=["qro"])
        p.barrier()

    if KLIM < 3:
        return
    for wname, dram, acc_c, m_c, eng, sl in (
            ("wg", g["wgS"], MACC, MG, "vec", 0),
            ("wu", g["wuS"], MACC + 1, MU, "vec", 1),
            ("wd", g["wdS"], MACC + 2, MD, "sca", 2)):
        flat = (dram.rearrange("a b c -> (a b c)") if wname == "wd"
                else dram.rearrange("a b c d -> (a b c d)"))
        tiles = flat.rearrange("(n p m) -> n p m", p=128, m=1024)
        acc = st(acc_c)
        kacc = f"macc_{wname}"
        p.op("vec", lambda v, acc=acc: v.memset(acc, 0.0), writes=[kacc])
        part = st(MPART + sl)
        kpart = f"mpart_{wname}"
        for i in range(128):
            wslot = wraw[:, (i % 2) * 1024:(i % 2) * 1024 + 1024]
            kw = f"wraw{(i % 2)}"
            p.dma("syn", wslot, tiles[i], writes=[kw])
            if eng == "sca":
                p.op("sca", lambda s, wslot=wslot, part=part: s.activation(
                    tq[:, 0:1024], wslot, AF.Abs, accum_out=part),
                    reads=[kw], writes=["tgu", kpart])
            else:
                p.op(eng, lambda v, wslot=wslot, part=part: v.reduce_sum(
                    part, wslot, axis=AX.X, apply_absolute_value=True),
                    reads=[kw], writes=[kpart])
            p.op("vec", lambda v, acc=acc, part=part: v.tensor_tensor(
                acc, acc, part, ALU.add), reads=[kacc, kpart], writes=[kacc])
        mcol = st(m_c)
        m1 = pool[0:1, 1536:1537]
        p.op("ten", lambda t, acc=acc: t.transpose(
            ps_c[0:1, 0:128], acc, c_idf[:]), reads=[kacc, "c_idf"], writes=["ps_c"])
        p.op("vec", lambda v, m1=m1: v.reduce_sum(m1, ps_c[0:1, 0:128], axis=AX.X),
             reads=["ps_c"], writes=["m1s"])
        p.op("vec", lambda v, m1=m1: v.tensor_scalar(
            m1, m1, 1.0 / 16777216.0, 1e-5, ALU.mult, ALU.max),
            reads=["m1s"], writes=["m1s"])
        p.op("ten", lambda t, m1=m1: t.matmul(
            ps_c[:, 0:1], c_of[0:1, :], m1, start=True, stop=True),
            reads=["m1s", "c_of"], writes=["ps_c"])
        p.op("vec", lambda v, mcol=mcol: v.tensor_copy(mcol, ps_c[:, 0:1]),
             reads=["ps_c"], writes=[f"m_{wname}"])
    for src, dsth, dstn in ((MG, MGH, MGN), (MU, MUH, MUN), (MD, MDH, MDN)):
        p.op("vec", lambda v, src=src, dsth=dsth: v.tensor_scalar(
            st(dsth), st(src), 0.5, None, ALU.mult),
            reads=[f"m_{'wg' if src == MG else 'wu' if src == MU else 'wd'}"],
            writes=[f"mh{dsth}"])
        p.op("vec", lambda v, src=src, dstn=dstn: v.tensor_scalar(
            st(dstn), st(src), -0.5, None, ALU.mult),
            reads=[f"m_{'wg' if src == MG else 'wu' if src == MU else 'wd'}"],
            writes=[f"mh{dstn}"])
    p.op("vec", lambda v: v.tensor_tensor(st(MUMD), st(MU), st(MD), ALU.mult),
         reads=["m_wu", "m_wd"], writes=["mumd"])
    p.op("vec", lambda v: v.tensor_scalar(
        st(MUMD), st(MUMD), 1.0 / math.sqrt(8192.0), None, ALU.mult),
        reads=["mumd"], writes=["mumd"])

    if KLIM < 4:
        return
    p.dma("syn", c_msk.rearrange("p (n q) -> p n q", q=128),
          g["maskT"].rearrange("n p q -> p n q"), writes=["c_msk"])

    evens = [h for h in range(H) if (h // 4) % 2 == 0]
    odds = [h for h in range(H) if (h // 4) % 2 == 1]
    qpos = {}
    for c in range(H // 2):
        qpos[evens[c]] = (c, 0)
        qpos[odds[c]] = (c, 1)
    for h in range(H):
        kvh = h // 4
        kc, kb64 = kvh // 2, (kvh % 2) * 64
        qc, qhalf = qpos[h]
        qb64 = qhalf * 64
        kav0, kav1 = "avc", "avd"
        for kb in range(NB):
            q0 = ilo(kb) * 128
            w = OT - q0
            sT, ksT = (ps_a, "ps_a") if (h * NB + kb) % 2 == 0 else (ps_b, "ps_b")

            def smm(t, kc=kc, kb64=kb64, kb=kb, qc=qc, qb64=qb64, q0=q0, w=w, sT=sT):
                r = None
                for j in range((w + 511) // 512):
                    n = min(512, w - j * 512)
                    r = t.matmul(
                        sT[:, j * 512:j * 512 + n],
                        big[kb64:kb64 + 64, KRO + kc * 2048 + kb * 128:
                            KRO + kc * 2048 + kb * 128 + 128],
                        big[qb64:qb64 + 64, QRO + qc * 1024 + q0 + j * 512:
                            QRO + qc * 1024 + q0 + j * 512 + n],
                        start=True, stop=True)
                return r
            p.op("ten", smm, reads=["kro", "qro"], writes=[ksT])
            prb = bg(PRB + (kb % 2) * 1024, 1024)
            kprb = f"prb{kb % 2}"
            p.op("sca", lambda s, sT=sT, prb=prb, w=w: s.activation(
                prb[:, 0:w], sT[:, 0:w], AF.Exp, bias=st(EXB), scale=0.125),
                reads=[ksT, "exb"], writes=[kprb])
            p.op("vec", lambda v, prb=prb, kb=kb: v.tensor_tensor(
                prb[:, 0:128], prb[:, 0:128],
                c_msk[:, kb * 128:kb * 128 + 128], ALU.mult),
                reads=[kprb, "c_msk"], writes=[kprb])

            def avmm(t, kb=kb, kvh=kvh, q0=q0, prb=prb):
                r = None
                for c in (0, 1):
                    lo = max(q0, c * 512)
                    if lo >= (c + 1) * 512:
                        continue
                    pmv = ps_c if c == 0 else ps_d
                    stop = (kb == NB - 1) if c == 1 else (kb == 7)
                    r = t.matmul(
                        pmv[0:65, lo - c * 512:512],
                        bg(VAU + kb * 520 + kvh * 65, 65),
                        prb[:, lo - q0:lo - q0 + 512 - (lo - c * 512)],
                        start=(kb == 0), stop=stop)
                return r
            p.op("ten", avmm, reads=[kprb, "vau"],
                 writes=([kav0] if ilo(kb) < 4 else []) + [kav1])

        for c in (0, 1):
            pmv, kav = (ps_c, "avc") if c == 0 else (ps_d, "avd")
            rrow2 = rrow[0:1, 0:512]
            p.op("vec", lambda v, pmv=pmv, rrow2=rrow2: v.reciprocal(
                rrow2, pmv[64:65, 0:512]), reads=[kav], writes=["rrow"])
            p.op("ten", lambda t, rrow2=rrow2: t.matmul(
                ps_a[0:64, 0:512], c_or[0:1, 0:64], rrow2, start=True, stop=True),
                reads=["rrow", "c_h"], writes=["ps_a"])
            p.op("sca", lambda s: s.copy(pool[0:64, 0:512], ps_a[0:64, 0:512]),
                 reads=["ps_a"], writes=["rbc"])
            p.op("vec", lambda v, pmv=pmv, h=h, qc=qc, qb64=qb64, c=c: v.tensor_tensor(
                big[qb64:qb64 + 64, YFM + qc * 1024 + c * 512:
                    YFM + qc * 1024 + c * 512 + 512],
                pmv[0:64, 0:512], pool[0:64, 0:512], ALU.mult),
                reads=[kav, "rbc"], writes=["yfm"])

    if KLIM < 5:
        return
    p.barrier()

    for part4 in range(4):
        p.dma("gps", bg(WOT + part4 * 8192, 8192).rearrange(
                  "p (dc v) -> p dc v", v=2048),
              g["woT"][part4 * 512:part4 * 512 + 512].rearrange(
                  "(dc p) v -> p dc v", p=128), writes=[f"wot{part4}"])
    for oc in range(DC):
        for tc in range(2):
            pm, kpm = (ps_c, "ps_c") if (oc * 2 + tc) % 2 == 0 else (ps_d, "ps_d")

            def omm(t, oc=oc, tc=tc, pm=pm):
                r = None
                for dc in range(DC):
                    r = t.matmul(pm[:, 0:512],
                                 bg(WOT + dc * 2048 + oc * 128, 128),
                                 bg(YFM + dc * 1024 + tc * 512, 512),
                                 start=(dc == 0), stop=(dc == DC - 1))
                return r
            p.op("ten", omm, reads=["wot0", "wot1", "wot2", "wot3", "yfm"],
                 writes=[kpm])
            p.op("sca", lambda s, pm=pm, oc=oc, tc=tc: s.copy(
                bg(AFM + oc * 1024 + tc * 512, 512), pm[:, 0:512]),
                reads=[kpm], writes=["afm"])

    p.barrier()

    x1r = g["x1_park"].rearrange("(n p) d -> n p d", p=128)
    outr = g["out"].rearrange("(n p) d -> n p d", p=128)
    for i in range(OB):
        for dc in range(DC):
            pt = (pb0 if dc % 2 == 0 else pb1)[:, 0:128]
            kp = f"pb{dc % 2}"
            p.op("ten", lambda t, dc=dc, i=i, pt=pt: t.transpose(
                pt, bg(AFM + dc * 1024 + i * 128, 128), c_id[:]),
                reads=["afm", "c_id"], writes=[kp])
            p.op("sca", lambda s, pt=pt, dc=dc: s.copy(bg(ATM + dc * 128, 128), pt),
                 reads=[kp], writes=["atm"])
        mu2, rs2, kmu2, krs2 = ln_stats(bg(ATM, 2048), D, SA, "atm",
                                        bg(X1S, 2048), "x1s")
        xo = pl(PXO, 2048)
        p.dma("syn", xo, g["x_seq"][i * 256:i * 256 + 128, :], writes=["xo"])
        p.op("vec", lambda v, mu2=mu2, rs2=rs2: v.tensor_scalar(
            bg(X1S, 2048), bg(ATM, 2048), mu2, rs2, ALU.subtract, ALU.mult),
            reads=["atm", kmu2, krs2], writes=["x1s"])
        p.op("vec", lambda v, xo=xo: v.tensor_tensor(
            bg(X1S, 2048), bg(X1S, 2048), xo, ALU.add),
            reads=["x1s", "xo"], writes=["x1s"])
        p.dma("syn", x1r[i], bg(X1S, 2048), reads=["x1s"], writes=[f"x1p{i}"])
        mu3, rs3, kmu3, krs3 = ln_stats(bg(X1S, 2048), D, SB, "x1s",
                                        bg(QXS, 2048), "qxs")
        mxp, mxn = st(SA + 5), st(SA + 6)
        p.op("vec", lambda v, mxp=mxp: v.reduce_max(mxp, bg(X1S, 2048), axis=AX.X),
             reads=["x1s"], writes=["mxp"])
        p.op("vec", lambda v, mxn=mxn: v.tensor_reduce(
            mxn, bg(X1S, 2048), axis=AX.X, op=ALU.min),
            reads=["x1s"], writes=["mxn"])
        p.op("vec", lambda v, mxp=mxp, mu3=mu3: v.tensor_tensor(
            mxp, mxp, mu3, ALU.subtract), reads=["mxp", kmu3], writes=["mxp"])
        p.op("vec", lambda v, mxn=mxn, mu3=mu3: v.tensor_tensor(
            mxn, mu3, mxn, ALU.subtract), reads=["mxn", kmu3], writes=["mxn"])
        p.op("vec", lambda v, mxp=mxp, mxn=mxn: v.tensor_tensor(
            mxp, mxp, mxn, ALU.max), reads=["mxp", "mxn"], writes=["mxp"])
        p.op("vec", lambda v, mxp=mxp, rs3=rs3: v.tensor_tensor(
            mxp, mxp, rs3, ALU.mult), reads=["mxp", krs3], writes=["mxp"])
        p.op("vec", lambda v, mxp=mxp: v.tensor_scalar(
            mxp, mxp, 1e-5, None, ALU.max), reads=["mxp"], writes=["mxp"])
        dq3 = st(DQ3 + i)
        p.op("vec", lambda v, mxp=mxp, dq3=dq3: v.tensor_scalar(
            dq3, mxp, 1.0 / 128.0, None, ALU.mult),
            reads=["mxp"], writes=[f"dq3_{i}"])
        s3 = st(SA + 7)
        p.op("vec", lambda v, mxp=mxp, s3=s3: v.reciprocal(s3, mxp),
             reads=["mxp"], writes=["s3"])
        p.op("vec", lambda v, s3=s3: v.tensor_scalar(s3, s3, 128.0, None, ALU.mult),
             reads=["s3"], writes=["s3"])
        rcq = st(SA + 5)
        p.op("vec", lambda v, rcq=rcq, rs3=rs3, s3=s3: v.tensor_tensor(
            rcq, rs3, s3, ALU.mult), reads=[krs3, "s3"], writes=["rcq"])
        zq = pl(PT0, 2048)
        p.op("vec", lambda v, zq=zq, mu3=mu3, rcq=rcq: v.tensor_scalar(
            zq, bg(X1S, 2048), mu3, rcq, ALU.subtract, ALU.mult),
            reads=["x1s", kmu3, "rcq"], writes=["zq"])
        ladder(zq[:, 0:1024], "zq", bg(QXS, 1024), "qxs", 1024)
        ladder(zq[:, 1024:2048], "zq", bg(QXS + 1024, 1024), "qxs", 1024)
        for dc in range(DC):
            pt = (pb0 if dc % 2 == 0 else pb1)[:, 0:128]
            kp = f"pb{dc % 2}"
            p.op("ten", lambda t, dc=dc, pt=pt: t.transpose(
                pt, bg(QXS + dc * 128, 128), c_id[:]),
                reads=["qxs", "c_id"], writes=[kp])
            p.op("sca", lambda s, pt=pt, dc=dc, i=i: s.copy(
                bg(QXT + dc * 1024 + i * 128, 128), pt),
                reads=[kp], writes=["qxt"])

    for i in range(OB):
        p.op("ten", lambda t, i=i: t.transpose(
            ps_c[0:1, 0:128], st(DQ3 + i), c_idf[:]),
            reads=[f"dq3_{i}", "c_idf"], writes=["ps_c"])
        p.op("vec", lambda v, i=i: v.tensor_copy(srow[0:1, i * 128:i * 128 + 128],
                                                 ps_c[0:1, 0:128]),
             reads=["ps_c"], writes=["srowd"])
    p.barrier()

    if KLIM < 6:
        return
    if "dbg_qxt" in g:
        p.dma("gps", g["dbg_qxt"], bg(QXT, 16384), reads=["qxt"], is_output=True)

    for tc in range(2):
        pm = ps_c if tc == 0 else ps_d

        def rmm(t, tc=tc, pm=pm):
            r = None
            for dc in range(DC):
                r = t.matmul(pm[0:1, 0:512], c_oc,
                             bg(QXT + dc * 1024 + tc * 512, 512),
                             start=(dc == 0), stop=(dc == DC - 1))
            return r
        p.op("ten", rmm, reads=["qxt", "c_h"], writes=["ps_c" if tc == 0 else "ps_d"])
        p.op("vec", lambda v, pm=pm, tc=tc: v.tensor_scalar(
            big[0:1, RS3 + tc * 512:RS3 + tc * 512 + 512], pm[0:1, 0:512],
            -1.0, None, ALU.mult),
            reads=["ps_c" if tc == 0 else "ps_d"], writes=["rs3row"])
    p.op("vec", lambda v: v.tensor_copy(big[0:1, 0:1024], srow[0:1, 0:1024]),
         reads=["srowd"], writes=["dq3bf"])
    for half in (0, 1):
        p.op("ten", lambda t, half=half: t.matmul(
            ps_c[:, 0:512], c_or[0:1, :],
            big[0:1, half * 512:half * 512 + 512], start=True, stop=True),
            reads=["dq3bf", "c_h"], writes=["ps_c"])
        p.op("vec", lambda v, half=half: v.tensor_copy(
            pl(PXA + half * 512, 512), ps_c[:, 0:512]),
            reads=["ps_c"], writes=["dqb"])

    for fc in range(FC):
        tgu = tq[:, 0:2048]
        for wname, dram, mh, mn, pm, kpm in (
                ("g", g["wgS"], MGH, MGN, ps_a, "ps_a"),
                ("u", g["wuS"], MUH, MUN, ps_b, "ps_b")):
            for hf in range(2):
                wslot = wraw[:, hf * 1024:hf * 1024 + 1024]
                kw = f"wraw{hf}"
                p.dma("syn", wslot.rearrange("p (dc f) -> p dc f", f=128),
                      dram[fc, hf * 8:hf * 8 + 8].rearrange("dc p f -> p dc f"),
                      writes=[kw])
                ca = u16[:, 0:1024]
                cb = c16[:, 0:1024]
                p.op("gps", lambda gq, wslot=wslot, ca=ca, mh=mh: gq.tensor_scalar(
                    ca, wslot, st(mh), None, ALU.is_ge),
                    reads=[kw, f"mh{mh}"], writes=["qc1"])
                p.op("vec", lambda v, wslot=wslot, cb=cb, mn=mn: v.tensor_scalar(
                    cb, wslot, st(mn), None, ALU.is_ge),
                    reads=[kw, f"mh{mn}"], writes=["qc2"])
                p.op("vec", lambda v, ca=ca, cb=cb, hf=hf: v.tensor_tensor(
                    tgu[:, hf * 1024:hf * 1024 + 1024], ca, cb, ALU.add),
                    reads=["qc1", "qc2"], writes=["tgu"])
            for tc in range(2):
                def gmm(t, tc=tc, pm=pm):
                    r = None
                    for dc in range(DC):
                        r = t.matmul(pm[:, tc * 512:tc * 512 + 512],
                                     tgu[:, dc * 128:dc * 128 + 128],
                                     bg(QXT + dc * 1024 + tc * 512, 512),
                                     start=(dc == 0), stop=False)
                    r = t.matmul(pm[:, tc * 512:tc * 512 + 512], c_or[0:1, 0:128],
                                 big[0:1, RS3 + tc * 512:RS3 + tc * 512 + 512],
                                 start=False, stop=True)
                    return r
                p.op("ten", gmm, reads=["tgu", "qxt", "rs3row", "c_h"], writes=[kpm])
        for tc in range(2):
            gt0 = pl(PXB + tc * 512, 512)
            gt1 = pl(PXB + 1024 + tc * 512, 512)
            p.op("vec", lambda v, tc=tc, gt0=gt0: v.tensor_tensor(
                gt0, ps_a[:, tc * 512:tc * 512 + 512], pl(PXA + tc * 512, 512),
                ALU.mult), reads=["ps_a", "dqb"], writes=["gt0"])
            p.op("sca", lambda s, gt0=gt0, gt1=gt1: s.activation(
                gt1, gt0, AF.Silu, scale=st(MG)),
                reads=["gt0", "m_wg"], writes=["gt1"])
            itile = tq[:, 0:512]
            p.op("vec", lambda v, tc=tc, gt1=gt1, itile=itile: v.tensor_tensor(
                itile, gt1, ps_b[:, tc * 512:tc * 512 + 512], ALU.mult),
                reads=["gt1", "ps_b"], writes=["int", "tgu"])
            pm, kpm = (ps_c, "ps_c") if tc == 0 else (ps_d, "ps_d")
            p.op("ten", lambda t, itile=itile, pm=pm: t.matmul(
                pm[:, 0:512], c_h[:], itile, start=True, stop=True),
                reads=["int", "tgu", "c_h", kpm], writes=[kpm])
            p.op("sca", lambda s, pm=pm, fc=fc, tc=tc: s.copy(
                bg(S1O + fc * 1024 + tc * 512, 512), pm[:, 0:512]),
                reads=[kpm], writes=["s1"])

    p.barrier()

    if KLIM < 7:
        return
    s1all = bg(S1O, 65536)
    for grp in range(32):
        w0 = grp * 32
        for b in range(6):
            period, half = 2 << b, 1 << b
            nhi = 64 // period
            hi_split = (period * 1024) > 16384
            for hi in (range(nhi) if hi_split else (None,)):
                if hi is None:
                    sview = s1all.rearrange("p (hi sub t) -> p hi sub t",
                                            sub=period, t=1024)
                    wview = tq[:, 0:2048].rearrange("p (hi sub t) -> p hi sub t",
                                                    sub=period, t=32)
                    s_lo = sview[:, :, 0:half, w0:w0 + 32]
                    s_hi = sview[:, :, half:period, w0:w0 + 32]
                    w_lo = wview[:, :, 0:half, :]
                    w_hi = wview[:, :, half:period, :]
                else:
                    sbase = bg(S1O + hi * period * 1024, period * 1024).rearrange(
                        "p (sub t) -> p sub t", t=1024)
                    wbase = tq[:, hi * period * 32:(hi + 1) * period * 32].rearrange(
                        "p (sub t) -> p sub t", t=32)
                    s_lo = sbase[:, 0:half, w0:w0 + 32]
                    s_hi = sbase[:, half:period, w0:w0 + 32]
                    w_lo = wbase[:, 0:half, :]
                    w_hi = wbase[:, half:period, :]
                if b % 2 == 0:
                    a_lo, a_hi, d_lo, d_hi, rk, wk2 = s_lo, s_hi, w_lo, w_hi, "s1", "bflyW"
                else:
                    a_lo, a_hi, d_lo, d_hi, rk, wk2 = w_lo, w_hi, s_lo, s_hi, "bflyW", "s1"
                p.op("vec", lambda v, a=a_lo, bb=a_hi, o=d_lo: v.tensor_tensor(
                    o, a, bb, ALU.add), reads=[rk], writes=[wk2])
                p.op("vec", lambda v, a=a_lo, bb=a_hi, o=d_hi: v.tensor_tensor(
                    o, a, bb, ALU.subtract), reads=[rk], writes=[wk2])
        srcr = s1all.rearrange("p (c t) -> p t c", t=1024)[:, w0:w0 + 32, :]
        p.op("vec", lambda v, srcr=srcr, w0=w0: v.tensor_reduce(
            pl(PXA + w0, 32), srcr, axis=AX.X,
            op=ALU.max, apply_absolute_value=True),
            reads=["s1"], writes=["acc"])
    for i in range(OB):
        p.op("ten", lambda t, i=i: t.transpose(
            ps_c[:, 0:128], pl(PXA, 1024)[:, i * 128:i * 128 + 128], c_idf[:]),
            reads=["acc", "c_idf"], writes=["ps_c"])
        mx, dq2, sc = st(SA), st(SA + 1), st(SA + 2)
        p.op("vec", lambda v, mx=mx: v.reduce_max(
            mx, ps_c[:, 0:128], axis=AX.X, apply_absolute_value=True),
            reads=["ps_c"], writes=["f2mx"])
        p.op("vec", lambda v, mx=mx: v.tensor_scalar(mx, mx, 1e-5, None, ALU.max),
             reads=["f2mx"], writes=["f2mx"])
        p.op("vec", lambda v, mx=mx, dq2=dq2: v.tensor_scalar(
            dq2, mx, 1.0 / 128.0, None, ALU.mult), reads=["f2mx"], writes=["f2dq2"])
        p.op("vec", lambda v, dq2=dq2, i=i: v.tensor_tensor(
            st(GCL + i), dq2, st(DQ3 + i), ALU.mult),
            reads=["f2dq2", f"dq3_{i}"], writes=[f"gc_{i}"])
        p.op("vec", lambda v, i=i: v.tensor_tensor(
            st(GCL + i), st(GCL + i), st(MUMD), ALU.mult),
            reads=[f"gc_{i}", "mumd"], writes=[f"gc_{i}"])
        p.op("vec", lambda v, mx=mx, sc=sc: v.reciprocal(sc, mx),
             reads=["f2mx"], writes=["f2sc"])
        p.op("vec", lambda v, sc=sc: v.tensor_scalar(sc, sc, 128.0, None, ALU.mult),
             reads=["f2sc"], writes=["f2sc"])
        p.op("ten", lambda t, sc=sc: t.transpose(
            ps_d[0:1, 0:128], sc, c_idf[:]),
            reads=["f2sc", "c_idf"], writes=["ps_d"])
        p.op("vec", lambda v, i=i: v.tensor_copy(
            srow[0:1, i * 128:i * 128 + 128], ps_d[0:1, 0:128]),
            reads=["ps_d"], writes=["f2srow"])
    p.op("vec", lambda v: v.tensor_copy(
        big[0:1, RS3:RS3 + 1024], srow[0:1, 0:1024]),
        reads=["f2srow"], writes=["srowbf"])
    for half in (0, 1):
        p.op("ten", lambda t, half=half: t.matmul(
            ps_c[:, 0:512], c_or[0:1, :],
            big[0:1, RS3 + half * 512:RS3 + half * 512 + 512],
            start=True, stop=True), reads=["srowbf", "c_h"], writes=["ps_c"])
        p.op("vec", lambda v, half=half: v.tensor_copy(
            pl(PSB + half * 512, 512), ps_c[:, 0:512]),
            reads=["ps_c"], writes=["sbc"])

    for grp in range(64):
        w0 = grp * 16
        src3 = s1all.rearrange("p (c t) -> p c t", t=1024)[:, :, w0:w0 + 16]
        zq = pl(PXB, 1024).rearrange("p (c t) -> p c t", t=16)
        sbw = pl(PSB, 1024)[:, w0:w0 + 16].unsqueeze(1).broadcast_to([128, 64, 16])
        p.op("vec", lambda v, src3=src3, zq=zq, sbw=sbw: v.tensor_tensor(
            zq, src3, sbw, ALU.mult), reads=["s1", "sbc"], writes=["zqf"])
        zqf = pl(PXB, 1024)
        ladder(zqf, "zqf", tq[:, 0:1024], "qf", 1024)
        qv = tq[:, 0:1024].rearrange("p (c t) -> p c t", t=16)
        p.op("vec", lambda v, qv=qv, src3=src3: v.tensor_copy(src3, qv),
             reads=["qf"], writes=["s1"])

    for tc in range(2):
        pm, kpm = (ps_c, "ps_c") if tc == 0 else (ps_d, "ps_d")

        def rfm(t, tc=tc, pm=pm):
            r = None
            for mc in range(FC):
                r = t.matmul(pm[0:1, 0:512], c_oc,
                             bg(S1O + mc * 1024 + tc * 512, 512),
                             start=(mc == 0), stop=(mc == FC - 1))
            return r
        p.op("ten", rfm, reads=["s1", "c_h"], writes=[kpm])
        p.op("vec", lambda v, pm=pm, tc=tc: v.tensor_scalar(
            big[0:1, RS3 + tc * 512:RS3 + tc * 512 + 512], pm[0:1, 0:512],
            -1.0, None, ALU.mult),
            reads=[kpm], writes=["rsfrow"])

    if "dbg_s1" in g:
        for qq in range(4):
            p.dma("gps", g["dbg_s1"][:, qq * 16384:(qq + 1) * 16384],
                  bg(S1O + qq * 16384, 16384), reads=["s1"], is_output=True)
    p.barrier()

    if KLIM < 8:
        return
    ocgs = [(0, 3), (3, 3), (6, 3), (9, 3), (12, 2), (14, 2)]
    for gi, (oc0, no) in enumerate(ocgs):
        for mc in range(FC):
            td = tq[:, 0:2048]
            ktd = "td"
            if gi == 0:
                for hf in range(2):
                    wslot = wraw[:, hf * 1024:hf * 1024 + 1024]
                    kw = f"wraw{hf}"
                    p.dma("syn", wslot, g["wdS"][mc][:, hf * 1024:hf * 1024 + 1024],
                          writes=[kw])
                    ca, cb = u16[:, 0:1024], c16[:, 0:1024]
                    p.op("gps", lambda gq, wslot=wslot, ca=ca: gq.tensor_scalar(
                        ca, wslot, st(MDH), None, ALU.is_ge),
                        reads=[kw, f"mh{MDH}"], writes=["qc1"])
                    p.op("vec", lambda v, wslot=wslot, cb=cb: v.tensor_scalar(
                        cb, wslot, st(MDN), None, ALU.is_ge),
                        reads=[kw, f"mh{MDN}"], writes=["qc2"])
                    p.op("vec", lambda v, ca=ca, cb=cb, td=td, hf=hf: v.tensor_tensor(
                        td[:, hf * 1024:hf * 1024 + 1024], ca, cb, ALU.add),
                        reads=["qc1", "qc2"], writes=[ktd])
                p.dma("gps", g["td_park"][mc], td, reads=[ktd],
                      writes=[f"tdp{mc}"])
            else:
                p.dma("gps", td[:, 0:no * 128],
                      g["td_park"][mc][:, oc0 * 128:(oc0 + no) * 128],
                      reads=[f"tdp{mc}"], writes=[ktd])

            def dmm(t, gi=gi, oc0=oc0, no=no, mc=mc, td=td):
                r = None
                for j in range(no):
                    off = (oc0 + j) * 128 if gi == 0 else j * 128
                    for tc in range(2):
                        pmv = [ps_a[:, 0:512], ps_a[:, 512:1024],
                               ps_b[:, 0:512], ps_b[:, 512:1024],
                               ps_c[:, 0:512], ps_d[:, 0:512]][j * 2 + tc]
                        r = t.matmul(pmv, td[:, off:off + 128],
                                     bg(S1O + mc * 1024 + tc * 512, 512),
                                     start=(mc == 0), stop=False)
                return r
            p.op("ten", dmm, reads=[ktd, "s1"], writes=["psD"])
        def daug(t, no=no):
            r = None
            for j in range(no):
                for tc in range(2):
                    pmv = [ps_a[:, 0:512], ps_a[:, 512:1024],
                           ps_b[:, 0:512], ps_b[:, 512:1024],
                           ps_c[:, 0:512], ps_d[:, 0:512]][j * 2 + tc]
                    r = t.matmul(pmv, c_or[0:1, 0:128],
                                 big[0:1, RS3 + tc * 512:RS3 + tc * 512 + 512],
                                 start=False, stop=True)
            return r
        p.op("ten", daug, reads=["rsfrow", "c_nr", "psD"], writes=["psD"])
        for j in range(no):
            for tc in range(2):
                pmv = [ps_a[:, 0:512], ps_a[:, 512:1024],
                       ps_b[:, 0:512], ps_b[:, 512:1024],
                       ps_c[:, 0:512], ps_d[:, 0:512]][j * 2 + tc]
                p.op("sca", lambda s, pmv=pmv, oc=oc0 + j, tc=tc: s.copy(
                    bg(DFM + oc * 1024 + tc * 512, 512), pmv),
                    reads=["psD"], writes=["dfm"])

    p.barrier()
    if "dbg_dfm" in g:
        p.dma("gps", g["dbg_dfm"], bg(DFM, 16384), reads=["dfm"], is_output=True)

    if KLIM < 9:
        return
    for i in range(OB):
        for dc in range(DC):
            pt = (pb0 if dc % 2 == 0 else pb1)[:, 0:128]
            kp = f"pb{dc % 2}"
            p.op("ten", lambda t, dc=dc, i=i, pt=pt: t.transpose(
                pt, bg(DFM + dc * 1024 + i * 128, 128), c_id[:]),
                reads=["dfm", "c_id"], writes=[kp])
            p.op("sca", lambda s, pt=pt, dc=dc: s.copy(
                bg(0 + dc * 128, 128), pt),
                reads=[kp], writes=["dtm"])
        dtm = bg(0, 2048)
        mur, rsr, kmur, _ = ln_stats(dtm, D, SA, "dtm", pl(PXB, 2048), "ost")
        var = st(SA + 1)
        gcol = st(GCL + i)
        a1, a2, a3 = st(LN4), st(LN4 + 1), st(LN4 + 2)
        p.op("vec", lambda v, var=var, a1=a1: v.tensor_scalar(
            a1, var, -1e-5, None, ALU.add), reads=[f"st{SA}_ex"], writes=["l4v"])
        p.op("vec", lambda v, gcol=gcol, a2=a2: v.tensor_tensor(
            a2, gcol, gcol, ALU.mult), reads=[f"gc_{i}"], writes=["l4g2"])
        p.op("vec", lambda v, a1=a1, a2=a2: v.tensor_tensor(
            a1, a1, a2, ALU.mult), reads=["l4v", "l4g2"], writes=["l4v"])
        p.op("vec", lambda v, a1=a1: v.tensor_scalar(
            a1, a1, 1e-5, None, ALU.add), reads=["l4v"], writes=["l4v"])
        p.op("sca", lambda s, a1=a1, a3=a3: s.activation(a3, a1, AF.Sqrt),
             reads=["l4v"], writes=["l4s"])
        rstd = st(LN4 + 3)
        p.op("vec", lambda v, a3=a3, rstd=rstd: v.reciprocal(rstd, a3),
             reads=["l4s"], writes=["l4r"])
        p.op("vec", lambda v, rstd=rstd, a3=a3: v.tensor_tensor(
            a3, rstd, rstd, ALU.mult), reads=["l4r"], writes=["l4s"])
        p.op("vec", lambda v, a3=a3, a1=a1: v.tensor_tensor(
            a3, a3, a1, ALU.mult), reads=["l4s", "l4v"], writes=["l4s"])
        p.op("vec", lambda v, a3=a3: v.tensor_scalar(
            a3, a3, -0.5, 1.5, ALU.mult, ALU.add), reads=["l4s"], writes=["l4s"])
        p.op("vec", lambda v, rstd=rstd, a3=a3: v.tensor_tensor(
            rstd, rstd, a3, ALU.mult), reads=["l4r", "l4s"], writes=["l4r"])
        sA_, sB_ = st(LN4 + 4), st(LN4 + 5)
        p.op("vec", lambda v, gcol=gcol, rstd=rstd, sA_=sA_: v.tensor_tensor(
            sA_, gcol, rstd, ALU.mult), reads=[f"gc_{i}", "l4r"], writes=["l4a"])
        p.op("vec", lambda v, sA_=sA_, mur=mur, sB_=sB_: v.tensor_tensor(
            sB_, sA_, mur, ALU.mult), reads=["l4a", kmur], writes=["l4b"])
        p.op("vec", lambda v, sB_=sB_: v.tensor_scalar(
            sB_, sB_, -1.0, None, ALU.mult), reads=["l4b"], writes=["l4b"])
        ost = pl(PXB, 2048)
        p.op("vec", lambda v, dtm=dtm, sA_=sA_, sB_=sB_, ost=ost: v.tensor_scalar(
            ost, dtm, sA_, sB_, ALU.mult, ALU.add),
            reads=["dtm", "l4a", "l4b"], writes=["ost"])
        p.dma("syn", tq[:, 0:2048], x1r[i], reads=[f"x1p{i}"], writes=["x1t"])
        p.op("vec", lambda v, ost=ost: v.tensor_tensor(
            ost, ost, tq[:, 0:2048], ALU.add), reads=["ost", "x1t"], writes=["ost"])
        p.dma("syn", outr[i], ost, reads=["ost"], writes=["outd"], is_output=True)



def kernel(**inputs):
    sh, cores = host_prep(inputs)
    nc = build_nc()
    names = ["wqT", "wkT", "wvT", "woT", "wgS", "wuS", "wdS", "ident",
             "ident32", "onesf", "h128", "p2"]
    in_maps = []
    for core in range(8):
        d = {k: sh[k] for k in names}
        for k in ("x_seq", "c_q", "s_q", "c_kp", "s_kp", "maskT"):
            d[k] = cores[core][k]
        in_maps.append(d)
    res = run_bass_kernel_spmd(nc, in_maps, core_ids=list(range(8)))
    outp = np.zeros((B, T, D), dtype=np.float32)
    for core in range(8):
        outp[core // 2, cores[core]["own_tok"]] = np.asarray(res.results[core]["out"])
    return outp


# version 06
# speedup vs baseline: 1.0661x; 1.0661x over previous
import contextlib
import math
import os

KLIM = int(os.environ.get("KLIM", "99"))

import numpy as np
import ml_dtypes

import concourse.bass as bass
import concourse.mybir as mybir
from concourse.bass_utils import run_bass_kernel_spmd

F32 = mybir.dt.float32
BF16 = mybir.dt.bfloat16
FP16 = mybir.dt.float16
FP8 = mybir.dt.float8e4
ALU = mybir.AluOpType
AF = mybir.ActivationFunctionType
AX = mybir.AxisListType
RED = None
BF = ml_dtypes.bfloat16

B, T, D = 4, 2048, 2048
H, HKV, HD = 32, 8, 64
FF = 8192
NB, OB, OT = 16, 8, 1024
DC, FC = 16, 64
EXPB = -20.0

ENGINES = ("ten", "vec", "sca", "gps", "syn")
DMA_QUEUES = ("syn", "gps", "sca")
N_SLOTS = 6


def ilo(kb):
    return kb // 2



def _ff_map():
    m = np.arange(FF)
    return (m % 128) * 64 + (m // 128)


def _hadamard128():
    i = np.arange(128)
    a = i[:, None] & i[None, :]
    pc = np.zeros((128, 128), dtype=np.int64)
    for b in range(8):
        pc += (a >> b) & 1
    return np.where(pc % 2 == 0, 1.0, -1.0).astype(BF)


_ROPE_CACHE = {}


def _rope_full_tables():
    if "cs" not in _ROPE_CACHE:
        import jax
        import jax.numpy as jnp
        with jax.default_device(jax.devices("cpu")[0]):
            theta = 1.0 / 10000.0 ** (jnp.arange(0, HD, 2, dtype=jnp.float32) / HD)
            t = jnp.arange(T, dtype=jnp.float32)
            freqs = t[:, None] * theta[None, :]
            cos = np.asarray(jax.device_get(jnp.cos(freqs)))
            sin = np.asarray(jax.device_get(jnp.sin(freqs)))
        _ROPE_CACHE["cs"] = (cos.T.astype(np.float32), sin.T.astype(np.float32))
    return _ROPE_CACHE["cs"]


def _rope_tables(positions):
    cos_all, sin_all = _rope_full_tables()
    pos = np.asarray(positions, dtype=np.int64)
    cos, sin = cos_all[:, pos], sin_all[:, pos]
    C = np.zeros((128, len(positions)), dtype=np.float64)
    S = np.zeros((128, len(positions)), dtype=np.float64)
    for half in range(2):
        r = half * 64
        C[r + 0:r + 64:2] = cos
        C[r + 1:r + 64:2] = cos
        S[r + 0:r + 64:2] = -sin
        S[r + 1:r + 64:2] = sin
    return C.astype(np.float32), S.astype(np.float32)


def _p2_swap():
    P = np.zeros((128, 128), dtype=np.float32)
    for half in range(2):
        r = half * 64
        for j in range(32):
            P[r + 2 * j, r + 2 * j + 1] = 1.0
            P[r + 2 * j + 1, r + 2 * j] = 1.0
    return P.astype(BF)


def host_prep(inputs):
    x = np.asarray(inputs["L0_attn_x"], dtype=np.float32)
    wq = np.asarray(inputs["L0_attn_wq"], dtype=np.float32)
    wk = np.asarray(inputs["L0_attn_wk"], dtype=np.float32)
    wv = np.asarray(inputs["L0_attn_wv"], dtype=np.float32)
    wo = np.asarray(inputs["L0_attn_wo"], dtype=np.float32)
    wg = np.asarray(inputs["L0_ffn_w_gate"], dtype=np.float32)
    wu = np.asarray(inputs["L0_ffn_w_up"], dtype=np.float32)
    wd = np.asarray(inputs["L0_ffn_w_down"], dtype=np.float32)

    d2 = _ff_map()
    sh = {}
    evens = [h for h in range(H) if (h // 4) % 2 == 0]
    odds = [h for h in range(H) if (h // 4) % 2 == 1]
    qcols = []
    for c in range(H // 2):
        qcols += list(range(evens[c] * HD, evens[c] * HD + HD))
        qcols += list(range(odds[c] * HD, odds[c] * HD + HD))
    qcols = np.array(qcols)
    sh["wqT"] = np.ascontiguousarray(wq.T[:, qcols])
    sh["wkT"] = np.ascontiguousarray(wk.T)
    sh["wvT"] = np.ascontiguousarray(wv.T)
    sh["woT"] = np.ascontiguousarray(wo.T[qcols, :])
    for name, w in (("wgS", wg), ("wuS", wu)):
        wt = w.T[:, d2]
        sh[name] = np.ascontiguousarray(
            wt.reshape(DC, 128, FC, 128).transpose(2, 0, 1, 3))
    sh["wdS"] = np.ascontiguousarray(wd.T[d2, :].reshape(FC, 128, D))
    sh["ident"] = np.eye(128, dtype=BF)
    sh["ident32"] = np.eye(128, dtype=np.float32)
    sh["onesf"] = np.ones((1, 128), dtype=np.float32)
    sh["h128"] = _hadamard128()
    sh["p2"] = _p2_swap()

    cores = []
    for core in range(8):
        s, par = core // 2, core % 2
        perm = np.arange(NB)
        if par == 1:
            perm = perm.reshape(-1, 2)[:, ::-1].ravel()
        tok_perm = (perm[:, None] * 128 + np.arange(128)[None, :]).ravel()
        own_tok = tok_perm.reshape(NB, 128)[0::2].ravel()
        d = {}
        d["x_seq"] = np.ascontiguousarray(x[s][tok_perm])
        d["c_q"], d["s_q"] = _rope_tables(own_tok)
        d["c_kp"], d["s_kp"] = _rope_tables(tok_perm)
        mk = np.ones((NB, 128, 128), dtype=np.float32)
        for kb in range(NB):
            i = ilo(kb)
            if kb == 2 * i:
                mk[kb] = np.tril(np.ones((128, 128), np.float32)).T
            elif par == 0:
                mk[kb] = 0.0
        d["maskT"] = mk.astype(BF)
        d["own_tok"] = own_tok
        cores.append(d)
    return sh, cores



class PB:
    def __init__(self, nc):
        self.nc = nc
        self.ops = {e: [] for e in ENGINES}
        self.cnt = {e: 0 for e in ENGINES}
        self.observed = {e: {} for e in ENGINES}
        self.writer = {}
        self.readers = {}
        self.slot_use = {q: [0] * N_SLOTS for q in DMA_QUEUES}
        self.slot_rr = {q: 0 for q in DMA_QUEUES}
        self.sems = {}
        self.out_tokens = []

    def alloc_sems(self, ctx):
        e = ctx.enter_context
        for en in ENGINES:
            self.sems["e_" + en] = e(self.nc.semaphore("e_" + en))
        for q in DMA_QUEUES:
            for s in range(N_SLOTS):
                self.sems[f"d_{q}_{s}"] = e(self.nc.semaphore(f"d_{q}_{s}"))


    def _filter(self, eng, toks):
        best = {}
        for sem, val in toks:
            if val > best.get(sem, 0):
                best[sem] = val
        waits = []
        for sem, val in best.items():
            if self.observed[eng].get(sem, 0) < val:
                waits.append((sem, val))
                self.observed[eng][sem] = val
        return waits

    def _deps(self, reads, writes):
        toks = []
        for b in reads:
            w = self.writer.get(b)
            if w is not None:
                toks.append(w)
        for b in writes:
            w = self.writer.get(b)
            if w is not None:
                toks.append(w)
            toks.extend(self.readers.get(b, ()))
        return toks

    def _update(self, tok, reads, writes):
        for b in writes:
            self.writer[b] = tok
            self.readers[b] = []
        for b in reads:
            if b not in writes:
                self.readers.setdefault(b, []).append(tok)

    def op(self, eng, fn, reads=(), writes=()):
        waits = self._filter(eng, self._deps(reads, writes))
        self.cnt[eng] += 1
        tok = ("e_" + eng, self.cnt[eng])
        self.ops[eng].append((waits, fn, tok))
        self._update(tok, reads, writes)
        return tok

    def dma(self, q, out_ap, in_ap, reads=(), writes=(), is_output=False):
        s = self.slot_rr[q]
        self.slot_rr[q] = (s + 1) % N_SLOTS
        sem = f"d_{q}_{s}"
        uses = self.slot_use[q][s]
        toks = ([(sem, 16 * uses)] if uses else []) + self._deps(reads, writes)
        waits = self._filter(q, toks)
        self.slot_use[q][s] = uses + 1
        tok = (sem, 16 * (uses + 1))

        def fn(eng, out_ap=out_ap, in_ap=in_ap):
            return eng.dma_start(out=out_ap, in_=in_ap)

        self.ops[q].append((waits, fn, tok))
        self._update(tok, reads, writes)
        if is_output:
            self.out_tokens.append(tok)
        return tok

    def barrier(self):
        toks = [("e_" + e2, self.cnt[e2]) for e2 in ENGINES if self.cnt[e2] > 0]
        for q in DMA_QUEUES:
            for s in range(N_SLOTS):
                if self.slot_use[q][s] > 0:
                    toks.append((f"d_{q}_{s}", 16 * self.slot_use[q][s]))
        for e in ENGINES:
            waits = self._filter(e, toks)
            if waits:
                self.ops[e].append((waits, None, None))

    def emit(self, block):
        sems = self.sems
        emitted = {}
        fin = {}
        for sem, val in self.out_tokens:
            q = sem.split("_")[1]
            cur = fin.setdefault(q, {})
            cur[sem] = max(cur.get(sem, 0), val)

        def mk(name):
            def body(eng):
                for waits, fn, tok in self.ops[name]:
                    for wsem, wval in waits:
                        eng.wait_ge(sems[wsem], wval)
                    if fn is None:
                        continue
                    sem, val = tok
                    ins = fn(eng)
                    ins.then_inc(sems[sem], val - emitted.get(sem, 0))
                    emitted[sem] = val
                for sem, val in fin.get(name, {}).items():
                    eng.wait_ge(sems[sem], val)
            return body

        block.tensor(mk("ten"))
        block.vector(mk("vec"))
        block.scalar(mk("sca"))
        block.gpsimd(mk("gps"))
        block.sync(mk("syn"))


XLT = 0
KRO = 32768
VAU = 40960
QRO = 49280
YFM = 65664
PRB = 0
WOT = 0
AFM = 32768
ATM = 49280
X1S = 51328
QXS = 53376
S1O = 0
QXT = 65664
DFM = 65664
RS3 = 82048
BIGN = 83072

PXA = 0
PXB = 2048
PCQ = 0
PSQ = 1024
PXO = 0
PT0 = 2048
PSB = 1024
PG = 3072
PGP = 1024
PN = 4096

STAT = 56
SA, SB = 0, 8
MG, MU, MD = 16, 17, 18
MGH, MUH = 19, 20
MGN, MUN = 21, 22
MDH, MDN = 23, 24
MACC = 25
MPART = 28
MUMD = 29
DQ3 = 32
GCL = 40
LN4 = 48
EXB = 55


def build_nc():
    nc = bass.Bass("TRN2", target_bir_lowering=False)
    g = {}

    def din(name, shape, dt=F32):
        g[name] = nc.dram_tensor(name, shape, dt, kind="ExternalInput").ap()

    din("x_seq", [T, D])
    din("wqT", [D, D])
    din("wkT", [D, HKV * HD])
    din("wvT", [D, HKV * HD])
    din("woT", [D, D])
    din("wgS", [FC, DC, 128, 128])
    din("wuS", [FC, DC, 128, 128])
    din("wdS", [FC, 128, D])
    din("ident", [128, 128], BF16)
    din("ident32", [128, 128])
    din("onesf", [1, 128])
    din("h128", [128, 128], BF16)
    din("p2", [128, 128], BF16)
    din("c_kp", [128, T])
    din("s_kp", [128, T])
    din("c_q", [128, OT])
    din("s_q", [128, OT])
    din("maskT", [NB, 128, 128], BF16)

    g["out"] = nc.dram_tensor("out", [OT, D], F32, kind="ExternalOutput").ap()
    import os as _os
    _dbg = _os.environ.get("KDBG", "") == "1"
    if _os.environ.get("KDBG2", "") == "1":
        g["dbg_qxt"] = nc.dram_tensor("dbg_qxt", [128, 16384], F32, kind="ExternalOutput").ap()
        g["dbg_s1"] = nc.dram_tensor("dbg_s1", [128, 65536], F32, kind="ExternalOutput").ap()
        g["dbg_dfm"] = nc.dram_tensor("dbg_dfm", [128, 16384], F32, kind="ExternalOutput").ap()
    g["x1_park"] = nc.dram_tensor("x1_park", [OT, D], BF16,
                                  kind="ExternalOutput" if _dbg else "Internal").ap()
    g["td_park"] = nc.dram_tensor("td_park", [FC, 128, D], FP8, kind="Internal").ap()

    with contextlib.ExitStack() as ctx:
        e = ctx.enter_context
        bufs = {}
        bufs["big"] = e(nc.sbuf_tensor("big", [128, BIGN], BF16))
        bufs["pool"] = e(nc.sbuf_tensor("pool", [128, PN], F32))
        bufs["u16"] = e(nc.sbuf_tensor("u16", [128, 1024], FP16))
        bufs["c16"] = e(nc.sbuf_tensor("c16", [128, 1024], FP16))
        bufs["wraw"] = e(nc.sbuf_tensor("wraw", [128, 2048], F32))
        bufs["tq"] = e(nc.sbuf_tensor("tq", [128, 2048], BF16))
        bufs["c_id"] = e(nc.sbuf_tensor("c_id", [128, 128], BF16))
        bufs["c_idf"] = e(nc.sbuf_tensor("c_idf", [128, 128], F32))
        bufs["c_of"] = e(nc.sbuf_tensor("c_of", [1, 128], F32))
        bufs["c_h"] = e(nc.sbuf_tensor("c_h", [128, 128], BF16))
        bufs["c_p2"] = e(nc.sbuf_tensor("c_p2", [128, 128], BF16))
        bufs["stat"] = e(nc.sbuf_tensor("stat", [128, STAT], F32))
        bufs["ps_a"] = e(nc.psum_tensor("ps_a", [128, 1024], F32))
        bufs["ps_b"] = e(nc.psum_tensor("ps_b", [128, 1024], F32))
        bufs["ps_c"] = e(nc.psum_tensor("ps_c", [128, 512], F32))
        bufs["ps_d"] = e(nc.psum_tensor("ps_d", [128, 512], F32))
        bufs["pb0"] = e(nc.psum_tensor("pb0", [128, 1024], BF16))
        bufs["pb1"] = e(nc.psum_tensor("pb1", [128, 1024], BF16))

        p = PB(nc)
        p.alloc_sems(ctx)
        with nc.allow_low_precision(reason="bf16 operands with fp32 psum accumulation; tolerance-sanctioned"):
            build_body(nc, p, g, bufs)
            blk = e(nc.Block())
            p.emit(blk)
    return nc


def build_body(nc, p, g, bufs):
    import concourse.bass_isa as bass_isa
    big = bufs["big"]; pool = bufs["pool"]; u16 = bufs["u16"]; c16 = bufs["c16"]
    wraw = bufs["wraw"]; tq = bufs["tq"]
    c_id = bufs["c_id"]; c_idf = bufs["c_idf"]; c_of = bufs["c_of"]
    c_h = bufs["c_h"]; c_p2 = bufs["c_p2"]
    stat = bufs["stat"]
    srow = bufs["pool"][0:1, 1024:2048]
    c_or = bufs["c_h"][0:1, :]
    c_oc = bufs["c_h"][:, 0:1]
    c_msk = bufs["big"][:, 4096:4096 + NB * 128]
    rrow = bufs["big"][0:1, 2048:2560]
    ps_a = bufs["ps_a"]; ps_b = bufs["ps_b"]; ps_c = bufs["ps_c"]
    ps_d = bufs["ps_d"]; pb0 = bufs["pb0"]; pb1 = bufs["pb1"]
    RMAX = bass_isa.ReduceOp.max

    def bg(off, n):
        return big[:, off:off + n]

    def pl(off, n):
        return pool[:, off:off + n]

    def st(c):
        return stat[:, c:c + 1]

    p.dma("syn", c_id[:], g["ident"], writes=["c_id"])
    p.dma("syn", c_idf[:], g["ident32"], writes=["c_idf"])
    p.dma("syn", c_of[:], g["onesf"], writes=["c_of"])
    p.op("vec", lambda v: v.memset(st(EXB), EXPB), writes=["exb"])
    p.dma("syn", c_h[:], g["h128"], writes=["c_h"])
    p.dma("syn", c_p2[:], g["p2"], writes=["c_p2"])


    if KLIM < 0:
        return
    def ln_stats(tile_ap, n, sbase, key_r, junk_ap, junk_key):
        mu, ex2, t2, sr, rs = (st(sbase + i) for i in (0, 1, 2, 3, 4))
        kb = f"st{sbase}_"
        p.op("vec", lambda v: v.reduce_sum(mu, tile_ap, axis=AX.X),
             reads=[key_r], writes=[kb + "mu"])
        p.op("sca", lambda s: s.activation(junk_ap, tile_ap, AF.Square, accum_out=ex2),
             reads=[key_r], writes=[junk_key, kb + "ex"])
        inv = 1.0 / n
        p.op("vec", lambda v: v.tensor_scalar(mu, mu, inv, None, ALU.mult),
             reads=[kb + "mu"], writes=[kb + "mu"])
        p.op("vec", lambda v: v.tensor_scalar(ex2, ex2, inv, None, ALU.mult),
             reads=[kb + "ex"], writes=[kb + "ex"])
        p.op("vec", lambda v: v.tensor_tensor(t2, mu, mu, ALU.mult),
             reads=[kb + "mu"], writes=[kb + "t2"])
        p.op("vec", lambda v: v.tensor_tensor(ex2, ex2, t2, ALU.subtract),
             reads=[kb + "ex", kb + "t2"], writes=[kb + "ex"])
        p.op("vec", lambda v: v.tensor_scalar(ex2, ex2, 1e-5, None, ALU.add),
             reads=[kb + "ex"], writes=[kb + "ex"])
        p.op("sca", lambda s: s.activation(sr, ex2, AF.Sqrt),
             reads=[kb + "ex"], writes=[kb + "sr"])
        p.op("vec", lambda v: v.reciprocal(rs, sr),
             reads=[kb + "sr"], writes=[kb + "rs"])
        p.op("vec", lambda v: v.tensor_tensor(sr, rs, rs, ALU.mult),
             reads=[kb + "rs"], writes=[kb + "sr"])
        p.op("vec", lambda v: v.tensor_tensor(sr, sr, ex2, ALU.mult),
             reads=[kb + "sr", kb + "ex"], writes=[kb + "sr"])
        p.op("vec", lambda v: v.tensor_scalar(sr, sr, -0.5, 1.5, ALU.mult, ALU.add),
             reads=[kb + "sr"], writes=[kb + "sr"])
        p.op("vec", lambda v: v.tensor_tensor(rs, rs, sr, ALU.mult),
             reads=[kb + "rs", kb + "sr"], writes=[kb + "rs"])
        return mu, rs, kb + "mu", kb + "rs"

    def ladder(zq_ap, kzq, out_ap, kout, n):
        u = u16[:, 0:n]
        c = c16[:, 0:n]
        p.op("vec", lambda v: v.tensor_scalar(u, zq_ap, 128.5, 1.0 / 512, ALU.add, ALU.mult),
             reads=[kzq], writes=["lad_u"])
        for _ in range(9):
            p.op("vec", lambda v: v.tensor_scalar(c, u, 0.5, None, ALU.is_ge),
                 reads=["lad_u"], writes=["lad_c"])
            p.op("vec", lambda v: v.scalar_tensor_tensor(u, u, 2.0, c, ALU.mult, ALU.subtract),
                 reads=["lad_u", "lad_c"], writes=["lad_u"])
        p.op("vec", lambda v: v.scalar_tensor_tensor(out_ap, zq_ap, 0.5, u, ALU.add, ALU.subtract),
             reads=[kzq, "lad_u"], writes=[kout])
        p.op("vec", lambda v: v.tensor_scalar(out_ap, out_ap, -128.0, 127.0, ALU.max, ALU.min),
             reads=[kout], writes=[kout])

    xs_d = g["x_seq"].rearrange("(n p) d -> n p d", p=128)
    for i in range(NB):
        par = i % 2
        xa = pl(PXA if par == 0 else PXB, 2048)
        kx = f"poolx{par}"
        p.dma("syn", xa, xs_d[i], writes=[kx])
        xb = tq[:, 0:2048] if par == 0 else bg(YFM, 2048)
        kxb = f"xb{par}"
        mu, rs, kmu, krs = ln_stats(xa, D, SA if par == 0 else SB, kx, xb, kxb)
        p.op("vec", lambda v, xa=xa, xb=xb, mu=mu, rs=rs: v.tensor_scalar(
            xb, xa, mu, rs, ALU.subtract, ALU.mult),
            reads=[kx, kmu, krs], writes=[kxb])
        for dc in range(DC):
            pt = (pb0 if dc % 2 == 0 else pb1)[:, 0:128]
            kp = f"pb{dc % 2}"
            p.op("ten", lambda t, xb=xb, pt=pt, dc=dc: t.transpose(
                pt, xb[:, dc * 128:dc * 128 + 128], c_id[:]),
                reads=[kxb, "c_id"], writes=[kp])
            p.op("sca", lambda s, pt=pt, dst=bg(XLT + dc * 2048 + i * 128, 128):
                 s.copy(dst, pt), reads=[kp], writes=["xlt"])

    if KLIM < 1:
        return
    p.dma("syn", pl(PXA, 2048), g["c_kp"], writes=["poolx0"])
    p.dma("syn", pl(PXB, 2048), g["s_kp"], writes=["poolx1"])

    p.dma("gps", bg(QRO, 8192).rearrange("p (dc v) -> p dc v", v=512),
          g["wkT"].rearrange("(dc p) v -> p dc v", p=128), writes=["wkb"])
    p.dma("gps", bg(QRO + 8192, 8192).rearrange("p (dc v) -> p dc v", v=512),
          g["wvT"].rearrange("(dc p) v -> p dc v", p=128), writes=["wvb"])

    for i in range(NB):
        pm, kpm = (ps_c, "ps_c") if i % 2 == 0 else (ps_d, "ps_d")

        def vmm(t, i=i, pm=pm):
            r = None
            for dc in range(DC):
                r = t.matmul(pm[:, 0:512], bg(XLT + dc * 2048 + i * 128, 128),
                             bg(QRO + 8192 + dc * 512, 512),
                             start=(dc == 0), stop=(dc == DC - 1))
            return r
        p.op("ten", vmm, reads=["xlt", "wvb"], writes=[kpm])
        vdst = bg(VAU + i * 520, 520).rearrange("p (h v) -> p h v", v=65)
        p.op("vec", lambda v, pm=pm, vdst=vdst: v.tensor_copy(
            vdst[:, :, 0:64], pm[:, 0:512].rearrange("p (h v) -> p h v", v=64)),
            reads=[kpm], writes=["vau"])
        p.op("vec", lambda v, vdst=vdst: v.memset(vdst[:, :, 64:65], 1.0),
             writes=["vau"])

    for kc in range(4):
        for tc in range(4):
            pm, kpm = (ps_c, "ps_c") if tc % 2 == 0 else (ps_d, "ps_d")

            def kmm(t, kc=kc, tc=tc, pm=pm):
                r = None
                for dc in range(DC):
                    r = t.matmul(pm[:, 0:512], bg(QRO + dc * 512 + kc * 128, 128),
                                 bg(XLT + dc * 2048 + tc * 512, 512),
                                 start=(dc == 0), stop=(dc == DC - 1))
                return r
            p.op("ten", kmm, reads=["xlt", "wkb"], writes=[kpm])
            kraw = tq[:, 0:512] if tc % 2 == 0 else tq[:, 512:1024]
            kkr = f"tqk{tc % 2}"
            p.op("sca", lambda s, pm=pm, kraw=kraw: s.copy(kraw, pm[:, 0:512]),
                 reads=[kpm], writes=[kkr])
            p.op("ten", lambda t, kraw=kraw, pm=pm: t.matmul(
                pm[:, 0:512], c_p2[:], kraw, start=True, stop=True),
                reads=[kkr, "c_p2", kpm], writes=[kpm])
            t1 = u16[:, 0:512]
            t2 = c16[:, 0:512]
            p.op("vec", lambda v, kraw=kraw, tc=tc, t1=t1: v.tensor_tensor(
                t1, kraw, pl(PXA + tc * 512, 512), ALU.mult),
                reads=[kkr, "poolx0"], writes=["ropet1"])
            p.op("vec", lambda v, pm=pm, tc=tc, t2=t2: v.tensor_tensor(
                t2, pm[:, 0:512], pl(PXB + tc * 512, 512), ALU.mult),
                reads=[kpm, "poolx1"], writes=["ropet2"])
            p.op("vec", lambda v, kc=kc, tc=tc, t1=t1, t2=t2: v.tensor_tensor(
                bg(KRO + kc * 2048 + tc * 512, 512), t1, t2, ALU.add),
                reads=["ropet1", "ropet2"], writes=["kro"])

    if KLIM < 2:
        return
    p.barrier()

    KSUB = int(os.environ.get("KSUB", "99"))
    p.dma("syn", pl(PCQ, 1024), g["c_q"], reads=["poolx0"], writes=["poolcq"])
    p.dma("syn", pl(PSQ, 1024), g["s_q"], reads=["poolx0", "poolx1"], writes=["poolsq"])
    for og in range(4):
        wqg = bg(YFM, 8192)
        p.dma("gps", wqg.rearrange("p (dc v) -> p dc v", v=512),
              g["wqT"][:, og * 512:og * 512 + 512].rearrange(
                  "(dc p) v -> p dc v", p=128), writes=["wqg"])
        for oc4 in range(4):
            oc = og * 4 + oc4
            for tc in range(2):
                pm, kpm = (ps_c, "ps_c") if (oc4 * 2 + tc) % 2 == 0 else (ps_d, "ps_d")

                def qmm(t, oc4=oc4, tc=tc, pm=pm):
                    r = None
                    for dc in range(DC):
                        mv = bg(XLT + dc * 2048 + tc * 1024, 1024).rearrange(
                            "p (b t) -> p b t", t=128)[:, 0:8:2, :]
                        r = t.matmul(pm[:, 0:512],
                                     bg(YFM + dc * 512 + oc4 * 128, 128), mv,
                                     start=(dc == 0), stop=(dc == DC - 1))
                    return r
                if KSUB < 1:
                    continue
                p.op("ten", qmm, reads=["xlt", "wqg"], writes=[kpm])
                if KSUB < 2:
                    continue
                qraw = tq[:, 0:512] if tc == 0 else tq[:, 512:1024]
                kqr = f"tqk{tc}"
                p.op("sca", lambda s, pm=pm, qraw=qraw: s.copy(qraw, pm[:, 0:512]),
                     reads=[kpm], writes=[kqr])
                if KSUB < 3:
                    continue
                p.op("ten", lambda t, qraw=qraw, pm=pm: t.matmul(
                    pm[:, 0:512], c_p2[:], qraw, start=True, stop=True),
                    reads=[kqr, "c_p2", kpm], writes=[kpm])
                t1 = u16[:, 0:512]
                t2 = c16[:, 0:512]
                p.op("vec", lambda v, qraw=qraw, tc=tc, t1=t1: v.tensor_tensor(
                    t1, qraw, pl(PCQ + tc * 512, 512), ALU.mult),
                    reads=[kqr, "poolcq"], writes=["ropet1"])
                p.op("vec", lambda v, pm=pm, tc=tc, t2=t2: v.tensor_tensor(
                    t2, pm[:, 0:512], pl(PSQ + tc * 512, 512), ALU.mult),
                    reads=[kpm, "poolsq"], writes=["ropet2"])
                p.op("vec", lambda v, oc=oc, tc=tc, t1=t1, t2=t2: v.tensor_tensor(
                    bg(QRO + oc * 1024 + tc * 512, 512), t1, t2, ALU.add),
                    reads=["ropet1", "ropet2"], writes=["qro"])
        p.barrier()

    if KLIM < 3:
        return
    for wname, dram, acc_c, m_c, eng, sl in (
            ("wg", g["wgS"], MACC, MG, "vec", 0),
            ("wu", g["wuS"], MACC + 1, MU, "vec", 1),
            ("wd", g["wdS"], MACC + 2, MD, "sca", 2)):
        flat = (dram.rearrange("a b c -> (a b c)") if wname == "wd"
                else dram.rearrange("a b c d -> (a b c d)"))
        tiles = flat.rearrange("(n p m) -> n p m", p=128, m=1024)
        acc = st(acc_c)
        kacc = f"macc_{wname}"
        p.op("vec", lambda v, acc=acc: v.memset(acc, 0.0), writes=[kacc])
        part = st(MPART + sl)
        kpart = f"mpart_{wname}"
        for i in range(128):
            wslot = wraw[:, (i % 2) * 1024:(i % 2) * 1024 + 1024]
            kw = f"wraw{(i % 2)}"
            p.dma("syn", wslot, tiles[i], writes=[kw])
            if eng == "sca":
                p.op("sca", lambda s, wslot=wslot, part=part: s.activation(
                    tq[:, 0:1024], wslot, AF.Abs, accum_out=part),
                    reads=[kw], writes=["tgu", kpart])
            else:
                p.op(eng, lambda v, wslot=wslot, part=part: v.reduce_sum(
                    part, wslot, axis=AX.X, apply_absolute_value=True),
                    reads=[kw], writes=[kpart])
            p.op("vec", lambda v, acc=acc, part=part: v.tensor_tensor(
                acc, acc, part, ALU.add), reads=[kacc, kpart], writes=[kacc])
        mcol = st(m_c)
        m1 = pool[0:1, 1536:1537]
        p.op("ten", lambda t, acc=acc: t.transpose(
            ps_c[0:1, 0:128], acc, c_idf[:]), reads=[kacc, "c_idf"], writes=["ps_c"])
        p.op("vec", lambda v, m1=m1: v.reduce_sum(m1, ps_c[0:1, 0:128], axis=AX.X),
             reads=["ps_c"], writes=["m1s"])
        p.op("vec", lambda v, m1=m1: v.tensor_scalar(
            m1, m1, 1.0 / 16777216.0, 1e-5, ALU.mult, ALU.max),
            reads=["m1s"], writes=["m1s"])
        p.op("ten", lambda t, m1=m1: t.matmul(
            ps_c[:, 0:1], c_of[0:1, :], m1, start=True, stop=True),
            reads=["m1s", "c_of"], writes=["ps_c"])
        p.op("vec", lambda v, mcol=mcol: v.tensor_copy(mcol, ps_c[:, 0:1]),
             reads=["ps_c"], writes=[f"m_{wname}"])
    for src, dsth, dstn in ((MG, MGH, MGN), (MU, MUH, MUN), (MD, MDH, MDN)):
        p.op("vec", lambda v, src=src, dsth=dsth: v.tensor_scalar(
            st(dsth), st(src), 0.5, None, ALU.mult),
            reads=[f"m_{'wg' if src == MG else 'wu' if src == MU else 'wd'}"],
            writes=[f"mh{dsth}"])
        p.op("vec", lambda v, src=src, dstn=dstn: v.tensor_scalar(
            st(dstn), st(src), -0.5, None, ALU.mult),
            reads=[f"m_{'wg' if src == MG else 'wu' if src == MU else 'wd'}"],
            writes=[f"mh{dstn}"])
    p.op("vec", lambda v: v.tensor_tensor(st(MUMD), st(MU), st(MD), ALU.mult),
         reads=["m_wu", "m_wd"], writes=["mumd"])
    p.op("vec", lambda v: v.tensor_scalar(
        st(MUMD), st(MUMD), 1.0 / math.sqrt(8192.0), None, ALU.mult),
        reads=["mumd"], writes=["mumd"])

    if KLIM < 4:
        return
    p.dma("syn", c_msk.rearrange("p (n q) -> p n q", q=128),
          g["maskT"].rearrange("n p q -> p n q"), writes=["c_msk"])

    evens = [h for h in range(H) if (h // 4) % 2 == 0]
    odds = [h for h in range(H) if (h // 4) % 2 == 1]
    qpos = {}
    for c in range(H // 2):
        qpos[evens[c]] = (c, 0)
        qpos[odds[c]] = (c, 1)
    for h in range(H):
        kvh = h // 4
        kc, kb64 = kvh // 2, (kvh % 2) * 64
        qc, qhalf = qpos[h]
        qb64 = qhalf * 64
        kav0, kav1 = "avc", "avd"
        for kb in range(NB):
            q0 = ilo(kb) * 128
            w = OT - q0
            sT, ksT = (ps_a, "ps_a") if (h * NB + kb) % 2 == 0 else (ps_b, "ps_b")

            def smm(t, kc=kc, kb64=kb64, kb=kb, qc=qc, qb64=qb64, q0=q0, w=w, sT=sT):
                r = None
                for j in range((w + 511) // 512):
                    n = min(512, w - j * 512)
                    r = t.matmul(
                        sT[:, j * 512:j * 512 + n],
                        big[kb64:kb64 + 64, KRO + kc * 2048 + kb * 128:
                            KRO + kc * 2048 + kb * 128 + 128],
                        big[qb64:qb64 + 64, QRO + qc * 1024 + q0 + j * 512:
                            QRO + qc * 1024 + q0 + j * 512 + n],
                        start=True, stop=True)
                return r
            p.op("ten", smm, reads=["kro", "qro"], writes=[ksT])
            prb = bg(PRB + (kb % 2) * 1024, 1024)
            kprb = f"prb{kb % 2}"
            p.op("sca", lambda s, sT=sT, prb=prb, w=w: s.activation(
                prb[:, 0:w], sT[:, 0:w], AF.Exp, bias=st(EXB), scale=0.125),
                reads=[ksT, "exb"], writes=[kprb])
            p.op("vec", lambda v, prb=prb, kb=kb: v.tensor_tensor(
                prb[:, 0:128], prb[:, 0:128],
                c_msk[:, kb * 128:kb * 128 + 128], ALU.mult),
                reads=[kprb, "c_msk"], writes=[kprb])

            def avmm(t, kb=kb, kvh=kvh, q0=q0, prb=prb):
                r = None
                for c in (0, 1):
                    lo = max(q0, c * 512)
                    if lo >= (c + 1) * 512:
                        continue
                    pmv = ps_c if c == 0 else ps_d
                    stop = (kb == NB - 1) if c == 1 else (kb == 7)
                    r = t.matmul(
                        pmv[0:65, lo - c * 512:512],
                        bg(VAU + kb * 520 + kvh * 65, 65),
                        prb[:, lo - q0:lo - q0 + 512 - (lo - c * 512)],
                        start=(kb == 0), stop=stop)
                return r
            p.op("ten", avmm, reads=[kprb, "vau"],
                 writes=([kav0] if ilo(kb) < 4 else []) + [kav1])

        for c in (0, 1):
            pmv, kav = (ps_c, "avc") if c == 0 else (ps_d, "avd")
            rrow2 = rrow[0:1, 0:512]
            p.op("vec", lambda v, pmv=pmv, rrow2=rrow2: v.reciprocal(
                rrow2, pmv[64:65, 0:512]), reads=[kav], writes=["rrow"])
            p.op("ten", lambda t, rrow2=rrow2: t.matmul(
                ps_a[0:64, 0:512], c_or[0:1, 0:64], rrow2, start=True, stop=True),
                reads=["rrow", "c_h"], writes=["ps_a"])
            p.op("sca", lambda s: s.copy(pool[0:64, 0:512], ps_a[0:64, 0:512]),
                 reads=["ps_a"], writes=["rbc"])
            p.op("vec", lambda v, pmv=pmv, h=h, qc=qc, qb64=qb64, c=c: v.tensor_tensor(
                big[qb64:qb64 + 64, YFM + qc * 1024 + c * 512:
                    YFM + qc * 1024 + c * 512 + 512],
                pmv[0:64, 0:512], pool[0:64, 0:512], ALU.mult),
                reads=[kav, "rbc"], writes=["yfm"])

    if KLIM < 5:
        return
    p.barrier()

    for part4 in range(4):
        p.dma("gps", bg(WOT + part4 * 8192, 8192).rearrange(
                  "p (dc v) -> p dc v", v=2048),
              g["woT"][part4 * 512:part4 * 512 + 512].rearrange(
                  "(dc p) v -> p dc v", p=128), writes=[f"wot{part4}"])
    for oc in range(DC):
        for tc in range(2):
            pm, kpm = (ps_c, "ps_c") if (oc * 2 + tc) % 2 == 0 else (ps_d, "ps_d")

            def omm(t, oc=oc, tc=tc, pm=pm):
                r = None
                for dc in range(DC):
                    r = t.matmul(pm[:, 0:512],
                                 bg(WOT + dc * 2048 + oc * 128, 128),
                                 bg(YFM + dc * 1024 + tc * 512, 512),
                                 start=(dc == 0), stop=(dc == DC - 1))
                return r
            p.op("ten", omm, reads=["wot0", "wot1", "wot2", "wot3", "yfm"],
                 writes=[kpm])
            p.op("sca", lambda s, pm=pm, oc=oc, tc=tc: s.copy(
                bg(AFM + oc * 1024 + tc * 512, 512), pm[:, 0:512]),
                reads=[kpm], writes=["afm"])

    p.barrier()

    x1r = g["x1_park"].rearrange("(n p) d -> n p d", p=128)
    outr = g["out"].rearrange("(n p) d -> n p d", p=128)
    for i in range(OB):
        for dc in range(DC):
            pt = (pb0 if dc % 2 == 0 else pb1)[:, 0:128]
            kp = f"pb{dc % 2}"
            p.op("ten", lambda t, dc=dc, i=i, pt=pt: t.transpose(
                pt, bg(AFM + dc * 1024 + i * 128, 128), c_id[:]),
                reads=["afm", "c_id"], writes=[kp])
            p.op("sca", lambda s, pt=pt, dc=dc: s.copy(bg(ATM + dc * 128, 128), pt),
                 reads=[kp], writes=["atm"])
        mu2, rs2, kmu2, krs2 = ln_stats(bg(ATM, 2048), D, SA, "atm",
                                        bg(X1S, 2048), "x1s")
        xo = pl(PXO, 2048)
        p.dma("syn", xo, g["x_seq"][i * 256:i * 256 + 128, :], writes=["xo"])
        p.op("vec", lambda v, mu2=mu2, rs2=rs2: v.tensor_scalar(
            bg(X1S, 2048), bg(ATM, 2048), mu2, rs2, ALU.subtract, ALU.mult),
            reads=["atm", kmu2, krs2], writes=["x1s"])
        p.op("vec", lambda v, xo=xo: v.tensor_tensor(
            bg(X1S, 2048), bg(X1S, 2048), xo, ALU.add),
            reads=["x1s", "xo"], writes=["x1s"])
        p.dma("syn", x1r[i], bg(X1S, 2048), reads=["x1s"], writes=[f"x1p{i}"])
        mu3, rs3, kmu3, krs3 = ln_stats(bg(X1S, 2048), D, SB, "x1s",
                                        bg(QXS, 2048), "qxs")
        mxp, mxn = st(SA + 5), st(SA + 6)
        p.op("vec", lambda v, mxp=mxp: v.reduce_max(mxp, bg(X1S, 2048), axis=AX.X),
             reads=["x1s"], writes=["mxp"])
        p.op("vec", lambda v, mxn=mxn: v.tensor_reduce(
            mxn, bg(X1S, 2048), axis=AX.X, op=ALU.min),
            reads=["x1s"], writes=["mxn"])
        p.op("vec", lambda v, mxp=mxp, mu3=mu3: v.tensor_tensor(
            mxp, mxp, mu3, ALU.subtract), reads=["mxp", kmu3], writes=["mxp"])
        p.op("vec", lambda v, mxn=mxn, mu3=mu3: v.tensor_tensor(
            mxn, mu3, mxn, ALU.subtract), reads=["mxn", kmu3], writes=["mxn"])
        p.op("vec", lambda v, mxp=mxp, mxn=mxn: v.tensor_tensor(
            mxp, mxp, mxn, ALU.max), reads=["mxp", "mxn"], writes=["mxp"])
        p.op("vec", lambda v, mxp=mxp, rs3=rs3: v.tensor_tensor(
            mxp, mxp, rs3, ALU.mult), reads=["mxp", krs3], writes=["mxp"])
        p.op("vec", lambda v, mxp=mxp: v.tensor_scalar(
            mxp, mxp, 1e-5, None, ALU.max), reads=["mxp"], writes=["mxp"])
        dq3 = st(DQ3 + i)
        p.op("vec", lambda v, mxp=mxp, dq3=dq3: v.tensor_scalar(
            dq3, mxp, 1.0 / 128.0, None, ALU.mult),
            reads=["mxp"], writes=[f"dq3_{i}"])
        s3 = st(SA + 7)
        p.op("vec", lambda v, mxp=mxp, s3=s3: v.reciprocal(s3, mxp),
             reads=["mxp"], writes=["s3"])
        p.op("vec", lambda v, s3=s3: v.tensor_scalar(s3, s3, 128.0, None, ALU.mult),
             reads=["s3"], writes=["s3"])
        rcq = st(SA + 5)
        p.op("vec", lambda v, rcq=rcq, rs3=rs3, s3=s3: v.tensor_tensor(
            rcq, rs3, s3, ALU.mult), reads=[krs3, "s3"], writes=["rcq"])
        zq = pl(PT0, 2048)
        p.op("vec", lambda v, zq=zq, mu3=mu3, rcq=rcq: v.tensor_scalar(
            zq, bg(X1S, 2048), mu3, rcq, ALU.subtract, ALU.mult),
            reads=["x1s", kmu3, "rcq"], writes=["zq"])
        ladder(zq[:, 0:1024], "zq", bg(QXS, 1024), "qxs", 1024)
        ladder(zq[:, 1024:2048], "zq", bg(QXS + 1024, 1024), "qxs", 1024)
        for dc in range(DC):
            pt = (pb0 if dc % 2 == 0 else pb1)[:, 0:128]
            kp = f"pb{dc % 2}"
            p.op("ten", lambda t, dc=dc, pt=pt: t.transpose(
                pt, bg(QXS + dc * 128, 128), c_id[:]),
                reads=["qxs", "c_id"], writes=[kp])
            p.op("sca", lambda s, pt=pt, dc=dc, i=i: s.copy(
                bg(QXT + dc * 1024 + i * 128, 128), pt),
                reads=[kp], writes=["qxt"])

    for i in range(OB):
        p.op("ten", lambda t, i=i: t.transpose(
            ps_c[0:1, 0:128], st(DQ3 + i), c_idf[:]),
            reads=[f"dq3_{i}", "c_idf"], writes=["ps_c"])
        p.op("vec", lambda v, i=i: v.tensor_copy(srow[0:1, i * 128:i * 128 + 128],
                                                 ps_c[0:1, 0:128]),
             reads=["ps_c"], writes=["srowd"])
    p.barrier()

    if KLIM < 6:
        return
    if "dbg_qxt" in g:
        p.dma("gps", g["dbg_qxt"], bg(QXT, 16384), reads=["qxt"], is_output=True)

    for tc in range(2):
        pm = ps_c if tc == 0 else ps_d

        def rmm(t, tc=tc, pm=pm):
            r = None
            for dc in range(DC):
                r = t.matmul(pm[0:1, 0:512], c_oc,
                             bg(QXT + dc * 1024 + tc * 512, 512),
                             start=(dc == 0), stop=(dc == DC - 1))
            return r
        p.op("ten", rmm, reads=["qxt", "c_h"], writes=["ps_c" if tc == 0 else "ps_d"])
        p.op("vec", lambda v, pm=pm, tc=tc: v.tensor_scalar(
            big[0:1, RS3 + tc * 512:RS3 + tc * 512 + 512], pm[0:1, 0:512],
            -1.0, None, ALU.mult),
            reads=["ps_c" if tc == 0 else "ps_d"], writes=["rs3row"])
    p.op("vec", lambda v: v.tensor_copy(big[0:1, 0:1024], srow[0:1, 0:1024]),
         reads=["srowd"], writes=["dq3bf"])
    for half in (0, 1):
        p.op("ten", lambda t, half=half: t.matmul(
            ps_c[:, 0:512], c_or[0:1, :],
            big[0:1, half * 512:half * 512 + 512], start=True, stop=True),
            reads=["dq3bf", "c_h"], writes=["ps_c"])
        p.op("vec", lambda v, half=half: v.tensor_copy(
            pl(PXA + half * 512, 512), ps_c[:, 0:512]),
            reads=["ps_c"], writes=["dqb"])

    for fc in range(FC):
        tgu = tq[:, 0:2048]
        for wname, dram, mh, mn, pm, kpm in (
                ("g", g["wgS"], MGH, MGN, ps_a, "ps_a"),
                ("u", g["wuS"], MUH, MUN, ps_b, "ps_b")):
            for hf in range(2):
                wslot = wraw[:, hf * 1024:hf * 1024 + 1024]
                kw = f"wraw{hf}"
                p.dma("syn", wslot.rearrange("p (dc f) -> p dc f", f=128),
                      dram[fc, hf * 8:hf * 8 + 8].rearrange("dc p f -> p dc f"),
                      writes=[kw])
                ca = u16[:, 0:1024]
                cb = c16[:, 0:1024]
                p.op("gps", lambda gq, wslot=wslot, ca=ca, mh=mh: gq.tensor_scalar(
                    ca, wslot, st(mh), None, ALU.is_ge),
                    reads=[kw, f"mh{mh}"], writes=["qc1"])
                p.op("vec", lambda v, wslot=wslot, cb=cb, mn=mn: v.tensor_scalar(
                    cb, wslot, st(mn), None, ALU.is_ge),
                    reads=[kw, f"mh{mn}"], writes=["qc2"])
                p.op("vec", lambda v, ca=ca, cb=cb, hf=hf: v.tensor_tensor(
                    tgu[:, hf * 1024:hf * 1024 + 1024], ca, cb, ALU.add),
                    reads=["qc1", "qc2"], writes=["tgu"])
            for tc in range(2):
                def gmm(t, tc=tc, pm=pm):
                    r = None
                    for dc in range(DC):
                        r = t.matmul(pm[:, tc * 512:tc * 512 + 512],
                                     tgu[:, dc * 128:dc * 128 + 128],
                                     bg(QXT + dc * 1024 + tc * 512, 512),
                                     start=(dc == 0), stop=False)
                    r = t.matmul(pm[:, tc * 512:tc * 512 + 512], c_or[0:1, 0:128],
                                 big[0:1, RS3 + tc * 512:RS3 + tc * 512 + 512],
                                 start=False, stop=True)
                    return r
                p.op("ten", gmm, reads=["tgu", "qxt", "rs3row", "c_h"], writes=[kpm])
        for tc in range(2):
            gt0 = pl(PXB + tc * 512, 512)
            gt1 = pl(PXB + 1024 + tc * 512, 512)
            p.op("vec", lambda v, tc=tc, gt0=gt0: v.tensor_tensor(
                gt0, ps_a[:, tc * 512:tc * 512 + 512], pl(PXA + tc * 512, 512),
                ALU.mult), reads=["ps_a", "dqb"], writes=["gt0"])
            p.op("sca", lambda s, gt0=gt0, gt1=gt1: s.activation(
                gt1, gt0, AF.Silu, scale=st(MG)),
                reads=["gt0", "m_wg"], writes=["gt1"])
            itile = tq[:, 0:512]
            p.op("vec", lambda v, tc=tc, gt1=gt1, itile=itile: v.tensor_tensor(
                itile, gt1, ps_b[:, tc * 512:tc * 512 + 512], ALU.mult),
                reads=["gt1", "ps_b"], writes=["int", "tgu"])
            pm, kpm = (ps_c, "ps_c") if tc == 0 else (ps_d, "ps_d")
            p.op("ten", lambda t, itile=itile, pm=pm: t.matmul(
                pm[:, 0:512], c_h[:], itile, start=True, stop=True),
                reads=["int", "tgu", "c_h", kpm], writes=[kpm])
            p.op("sca", lambda s, pm=pm, fc=fc, tc=tc: s.copy(
                bg(S1O + fc * 1024 + tc * 512, 512), pm[:, 0:512]),
                reads=[kpm], writes=["s1"])

    p.barrier()

    if KLIM < 7:
        return
    s1all = bg(S1O, 65536)
    for grp in range(32):
        w0 = grp * 32
        for b in range(6):
            period, half = 2 << b, 1 << b
            nhi = 64 // period
            hi_split = (period * 1024) > 16384
            for hi in (range(nhi) if hi_split else (None,)):
                if hi is None:
                    sview = s1all.rearrange("p (hi sub t) -> p hi sub t",
                                            sub=period, t=1024)
                    wview = tq[:, 0:2048].rearrange("p (hi sub t) -> p hi sub t",
                                                    sub=period, t=32)
                    s_lo = sview[:, :, 0:half, w0:w0 + 32]
                    s_hi = sview[:, :, half:period, w0:w0 + 32]
                    w_lo = wview[:, :, 0:half, :]
                    w_hi = wview[:, :, half:period, :]
                else:
                    sbase = bg(S1O + hi * period * 1024, period * 1024).rearrange(
                        "p (sub t) -> p sub t", t=1024)
                    wbase = tq[:, hi * period * 32:(hi + 1) * period * 32].rearrange(
                        "p (sub t) -> p sub t", t=32)
                    s_lo = sbase[:, 0:half, w0:w0 + 32]
                    s_hi = sbase[:, half:period, w0:w0 + 32]
                    w_lo = wbase[:, 0:half, :]
                    w_hi = wbase[:, half:period, :]
                if b % 2 == 0:
                    a_lo, a_hi, d_lo, d_hi, rk, wk2 = s_lo, s_hi, w_lo, w_hi, "s1", "bflyW"
                else:
                    a_lo, a_hi, d_lo, d_hi, rk, wk2 = w_lo, w_hi, s_lo, s_hi, "bflyW", "s1"
                p.op("vec", lambda v, a=a_lo, bb=a_hi, o=d_lo: v.tensor_tensor(
                    o, a, bb, ALU.add), reads=[rk], writes=[wk2])
                p.op("vec", lambda v, a=a_lo, bb=a_hi, o=d_hi: v.tensor_tensor(
                    o, a, bb, ALU.subtract), reads=[rk], writes=[wk2])
        srcr = s1all.rearrange("p (c t) -> p t c", t=1024)[:, w0:w0 + 32, :]
        p.op("vec", lambda v, srcr=srcr, w0=w0: v.tensor_reduce(
            pl(PXA + w0, 32), srcr, axis=AX.X,
            op=ALU.max, apply_absolute_value=True),
            reads=["s1"], writes=["acc"])
    for i in range(OB):
        p.op("ten", lambda t, i=i: t.transpose(
            ps_c[:, 0:128], pl(PXA, 1024)[:, i * 128:i * 128 + 128], c_idf[:]),
            reads=["acc", "c_idf"], writes=["ps_c"])
        mx, dq2, sc = st(SA), st(SA + 1), st(SA + 2)
        p.op("vec", lambda v, mx=mx: v.reduce_max(
            mx, ps_c[:, 0:128], axis=AX.X, apply_absolute_value=True),
            reads=["ps_c"], writes=["f2mx"])
        p.op("vec", lambda v, mx=mx: v.tensor_scalar(mx, mx, 1e-5, None, ALU.max),
             reads=["f2mx"], writes=["f2mx"])
        p.op("vec", lambda v, mx=mx, dq2=dq2: v.tensor_scalar(
            dq2, mx, 1.0 / 128.0, None, ALU.mult), reads=["f2mx"], writes=["f2dq2"])
        p.op("vec", lambda v, dq2=dq2, i=i: v.tensor_tensor(
            st(GCL + i), dq2, st(DQ3 + i), ALU.mult),
            reads=["f2dq2", f"dq3_{i}"], writes=[f"gc_{i}"])
        p.op("vec", lambda v, i=i: v.tensor_tensor(
            st(GCL + i), st(GCL + i), st(MUMD), ALU.mult),
            reads=[f"gc_{i}", "mumd"], writes=[f"gc_{i}"])
        p.op("vec", lambda v, mx=mx, sc=sc: v.reciprocal(sc, mx),
             reads=["f2mx"], writes=["f2sc"])
        p.op("vec", lambda v, sc=sc: v.tensor_scalar(sc, sc, 128.0, None, ALU.mult),
             reads=["f2sc"], writes=["f2sc"])
        p.op("ten", lambda t, sc=sc: t.transpose(
            ps_d[0:1, 0:128], sc, c_idf[:]),
            reads=["f2sc", "c_idf"], writes=["ps_d"])
        p.op("vec", lambda v, i=i: v.tensor_copy(
            srow[0:1, i * 128:i * 128 + 128], ps_d[0:1, 0:128]),
            reads=["ps_d"], writes=["f2srow"])
    p.op("vec", lambda v: v.tensor_copy(
        big[0:1, RS3:RS3 + 1024], srow[0:1, 0:1024]),
        reads=["f2srow"], writes=["srowbf"])
    for half in (0, 1):
        p.op("ten", lambda t, half=half: t.matmul(
            ps_c[:, 0:512], c_or[0:1, :],
            big[0:1, RS3 + half * 512:RS3 + half * 512 + 512],
            start=True, stop=True), reads=["srowbf", "c_h"], writes=["ps_c"])
        p.op("vec", lambda v, half=half: v.tensor_copy(
            pl(PSB + half * 512, 512), ps_c[:, 0:512]),
            reads=["ps_c"], writes=["sbc"])

    for grp in range(64):
        w0 = grp * 16
        src3 = s1all.rearrange("p (c t) -> p c t", t=1024)[:, :, w0:w0 + 16]
        zq = pl(PXB, 1024).rearrange("p (c t) -> p c t", t=16)
        sbw = pl(PSB, 1024)[:, w0:w0 + 16].unsqueeze(1).broadcast_to([128, 64, 16])
        p.op("vec", lambda v, src3=src3, zq=zq, sbw=sbw: v.tensor_tensor(
            zq, src3, sbw, ALU.mult), reads=["s1", "sbc"], writes=["zqf"])
        zqf = pl(PXB, 1024)
        ladder(zqf, "zqf", tq[:, 0:1024], "qf", 1024)
        qv = tq[:, 0:1024].rearrange("p (c t) -> p c t", t=16)
        p.op("vec", lambda v, qv=qv, src3=src3: v.tensor_copy(src3, qv),
             reads=["qf"], writes=["s1"])

    for tc in range(2):
        pm, kpm = (ps_c, "ps_c") if tc == 0 else (ps_d, "ps_d")

        def rfm(t, tc=tc, pm=pm):
            r = None
            for mc in range(FC):
                r = t.matmul(pm[0:1, 0:512], c_oc,
                             bg(S1O + mc * 1024 + tc * 512, 512),
                             start=(mc == 0), stop=(mc == FC - 1))
            return r
        p.op("ten", rfm, reads=["s1", "c_h"], writes=[kpm])
        p.op("vec", lambda v, pm=pm, tc=tc: v.tensor_scalar(
            big[0:1, RS3 + tc * 512:RS3 + tc * 512 + 512], pm[0:1, 0:512],
            -1.0, None, ALU.mult),
            reads=[kpm], writes=["rsfrow"])

    if "dbg_s1" in g:
        for qq in range(4):
            p.dma("gps", g["dbg_s1"][:, qq * 16384:(qq + 1) * 16384],
                  bg(S1O + qq * 16384, 16384), reads=["s1"], is_output=True)
    p.barrier()

    if KLIM < 8:
        return
    ocgs = [(0, 3), (3, 3), (6, 3), (9, 3), (12, 2), (14, 2)]
    for gi, (oc0, no) in enumerate(ocgs):
        for mc in range(FC):
            if gi == 0:
                td = tq[:, 0:2048]
                ktd = "td"
            else:
                td = tq[:, (mc % 2) * 1024:(mc % 2) * 1024 + 1024]
                ktd = f"td{mc % 2}"
            if gi == 0:
                for hf in range(2):
                    wslot = wraw[:, hf * 1024:hf * 1024 + 1024]
                    kw = f"wraw{hf}"
                    p.dma("syn", wslot, g["wdS"][mc][:, hf * 1024:hf * 1024 + 1024],
                          writes=[kw])
                    ca, cb = u16[:, 0:1024], c16[:, 0:1024]
                    p.op("gps", lambda gq, wslot=wslot, ca=ca: gq.tensor_scalar(
                        ca, wslot, st(MDH), None, ALU.is_ge),
                        reads=[kw, f"mh{MDH}"], writes=["qc1"])
                    p.op("vec", lambda v, wslot=wslot, cb=cb: v.tensor_scalar(
                        cb, wslot, st(MDN), None, ALU.is_ge),
                        reads=[kw, f"mh{MDN}"], writes=["qc2"])
                    p.op("vec", lambda v, ca=ca, cb=cb, td=td, hf=hf: v.tensor_tensor(
                        td[:, hf * 1024:hf * 1024 + 1024], ca, cb, ALU.add),
                        reads=["qc1", "qc2"], writes=[ktd])
                p.dma("gps", g["td_park"][mc], td, reads=[ktd],
                      writes=[f"tdp{mc}"])
            else:
                p.dma("gps", td[:, 0:no * 128],
                      g["td_park"][mc][:, oc0 * 128:(oc0 + no) * 128],
                      reads=[f"tdp{mc}"], writes=[ktd, "td"])

            def dmm(t, gi=gi, oc0=oc0, no=no, mc=mc, td=td):
                r = None
                for j in range(no):
                    off = (oc0 + j) * 128 if gi == 0 else j * 128
                    for tc in range(2):
                        pmv = [ps_a[:, 0:512], ps_a[:, 512:1024],
                               ps_b[:, 0:512], ps_b[:, 512:1024],
                               ps_c[:, 0:512], ps_d[:, 0:512]][j * 2 + tc]
                        r = t.matmul(pmv, td[:, off:off + 128],
                                     bg(S1O + mc * 1024 + tc * 512, 512),
                                     start=(mc == 0), stop=False)
                return r
            p.op("ten", dmm, reads=[ktd, "s1"], writes=["psD"])
        def daug(t, no=no):
            r = None
            for j in range(no):
                for tc in range(2):
                    pmv = [ps_a[:, 0:512], ps_a[:, 512:1024],
                           ps_b[:, 0:512], ps_b[:, 512:1024],
                           ps_c[:, 0:512], ps_d[:, 0:512]][j * 2 + tc]
                    r = t.matmul(pmv, c_or[0:1, 0:128],
                                 big[0:1, RS3 + tc * 512:RS3 + tc * 512 + 512],
                                 start=False, stop=True)
            return r
        p.op("ten", daug, reads=["rsfrow", "c_nr", "psD"], writes=["psD"])
        for j in range(no):
            for tc in range(2):
                pmv = [ps_a[:, 0:512], ps_a[:, 512:1024],
                       ps_b[:, 0:512], ps_b[:, 512:1024],
                       ps_c[:, 0:512], ps_d[:, 0:512]][j * 2 + tc]
                p.op("sca", lambda s, pmv=pmv, oc=oc0 + j, tc=tc: s.copy(
                    bg(DFM + oc * 1024 + tc * 512, 512), pmv),
                    reads=["psD"], writes=["dfm"])

    p.barrier()
    if "dbg_dfm" in g:
        p.dma("gps", g["dbg_dfm"], bg(DFM, 16384), reads=["dfm"], is_output=True)

    if KLIM < 9:
        return
    for i in range(OB):
        for dc in range(DC):
            pt = (pb0 if dc % 2 == 0 else pb1)[:, 0:128]
            kp = f"pb{dc % 2}"
            p.op("ten", lambda t, dc=dc, i=i, pt=pt: t.transpose(
                pt, bg(DFM + dc * 1024 + i * 128, 128), c_id[:]),
                reads=["dfm", "c_id"], writes=[kp])
            p.op("sca", lambda s, pt=pt, dc=dc: s.copy(
                bg(0 + dc * 128, 128), pt),
                reads=[kp], writes=["dtm"])
        dtm = bg(0, 2048)
        mur, rsr, kmur, _ = ln_stats(dtm, D, SA, "dtm", pl(PXB, 2048), "ost")
        var = st(SA + 1)
        gcol = st(GCL + i)
        a1, a2, a3 = st(LN4), st(LN4 + 1), st(LN4 + 2)
        p.op("vec", lambda v, var=var, a1=a1: v.tensor_scalar(
            a1, var, -1e-5, None, ALU.add), reads=[f"st{SA}_ex"], writes=["l4v"])
        p.op("vec", lambda v, gcol=gcol, a2=a2: v.tensor_tensor(
            a2, gcol, gcol, ALU.mult), reads=[f"gc_{i}"], writes=["l4g2"])
        p.op("vec", lambda v, a1=a1, a2=a2: v.tensor_tensor(
            a1, a1, a2, ALU.mult), reads=["l4v", "l4g2"], writes=["l4v"])
        p.op("vec", lambda v, a1=a1: v.tensor_scalar(
            a1, a1, 1e-5, None, ALU.add), reads=["l4v"], writes=["l4v"])
        p.op("sca", lambda s, a1=a1, a3=a3: s.activation(a3, a1, AF.Sqrt),
             reads=["l4v"], writes=["l4s"])
        rstd = st(LN4 + 3)
        p.op("vec", lambda v, a3=a3, rstd=rstd: v.reciprocal(rstd, a3),
             reads=["l4s"], writes=["l4r"])
        p.op("vec", lambda v, rstd=rstd, a3=a3: v.tensor_tensor(
            a3, rstd, rstd, ALU.mult), reads=["l4r"], writes=["l4s"])
        p.op("vec", lambda v, a3=a3, a1=a1: v.tensor_tensor(
            a3, a3, a1, ALU.mult), reads=["l4s", "l4v"], writes=["l4s"])
        p.op("vec", lambda v, a3=a3: v.tensor_scalar(
            a3, a3, -0.5, 1.5, ALU.mult, ALU.add), reads=["l4s"], writes=["l4s"])
        p.op("vec", lambda v, rstd=rstd, a3=a3: v.tensor_tensor(
            rstd, rstd, a3, ALU.mult), reads=["l4r", "l4s"], writes=["l4r"])
        sA_, sB_ = st(LN4 + 4), st(LN4 + 5)
        p.op("vec", lambda v, gcol=gcol, rstd=rstd, sA_=sA_: v.tensor_tensor(
            sA_, gcol, rstd, ALU.mult), reads=[f"gc_{i}", "l4r"], writes=["l4a"])
        p.op("vec", lambda v, sA_=sA_, mur=mur, sB_=sB_: v.tensor_tensor(
            sB_, sA_, mur, ALU.mult), reads=["l4a", kmur], writes=["l4b"])
        p.op("vec", lambda v, sB_=sB_: v.tensor_scalar(
            sB_, sB_, -1.0, None, ALU.mult), reads=["l4b"], writes=["l4b"])
        ost = pl(PXB, 2048)
        p.op("vec", lambda v, dtm=dtm, sA_=sA_, sB_=sB_, ost=ost: v.tensor_scalar(
            ost, dtm, sA_, sB_, ALU.mult, ALU.add),
            reads=["dtm", "l4a", "l4b"], writes=["ost"])
        p.dma("syn", tq[:, 0:2048], x1r[i], reads=[f"x1p{i}"], writes=["x1t"])
        p.op("vec", lambda v, ost=ost: v.tensor_tensor(
            ost, ost, tq[:, 0:2048], ALU.add), reads=["ost", "x1t"], writes=["ost"])
        p.dma("syn", outr[i], ost, reads=["ost"], writes=["outd"], is_output=True)



def kernel(**inputs):
    sh, cores = host_prep(inputs)
    nc = build_nc()
    names = ["wqT", "wkT", "wvT", "woT", "wgS", "wuS", "wdS", "ident",
             "ident32", "onesf", "h128", "p2"]
    in_maps = []
    for core in range(8):
        d = {k: sh[k] for k in names}
        for k in ("x_seq", "c_q", "s_q", "c_kp", "s_kp", "maskT"):
            d[k] = cores[core][k]
        in_maps.append(d)
    res = run_bass_kernel_spmd(nc, in_maps, core_ids=list(range(8)))
    outp = np.zeros((B, T, D), dtype=np.float32)
    for core in range(8):
        outp[core // 2, cores[core]["own_tok"]] = np.asarray(res.results[core]["out"])
    return outp


# version 07 (final)
# speedup vs baseline: 1.0700x; 1.0036x over previous
import contextlib
import math
import os

KLIM = int(os.environ.get("KLIM", "99"))

import numpy as np
import ml_dtypes

import concourse.bass as bass
import concourse.mybir as mybir
from concourse.bass_utils import run_bass_kernel_spmd

F32 = mybir.dt.float32
BF16 = mybir.dt.bfloat16
FP16 = mybir.dt.float16
FP8 = mybir.dt.float8e4
ALU = mybir.AluOpType
AF = mybir.ActivationFunctionType
AX = mybir.AxisListType
RED = None
BF = ml_dtypes.bfloat16

B, T, D = 4, 2048, 2048
H, HKV, HD = 32, 8, 64
FF = 8192
NB, OB, OT = 16, 8, 1024
DC, FC = 16, 64
EXPB = -20.0

ENGINES = ("ten", "vec", "sca", "gps", "syn")
DMA_QUEUES = ("syn", "gps", "sca")
N_SLOTS = 6


def ilo(kb):
    return kb // 2



def _ff_map():
    m = np.arange(FF)
    return (m % 128) * 64 + (m // 128)


def _hadamard128():
    i = np.arange(128)
    a = i[:, None] & i[None, :]
    pc = np.zeros((128, 128), dtype=np.int64)
    for b in range(8):
        pc += (a >> b) & 1
    return np.where(pc % 2 == 0, 1.0, -1.0).astype(BF)


_ROPE_CACHE = {}


def _rope_full_tables():
    if "cs" not in _ROPE_CACHE:
        import jax
        import jax.numpy as jnp
        with jax.default_device(jax.devices("cpu")[0]):
            theta = 1.0 / 10000.0 ** (jnp.arange(0, HD, 2, dtype=jnp.float32) / HD)
            t = jnp.arange(T, dtype=jnp.float32)
            freqs = t[:, None] * theta[None, :]
            cos = np.asarray(jax.device_get(jnp.cos(freqs)))
            sin = np.asarray(jax.device_get(jnp.sin(freqs)))
        _ROPE_CACHE["cs"] = (cos.T.astype(np.float32), sin.T.astype(np.float32))
    return _ROPE_CACHE["cs"]


def _rope_tables(positions):
    cos_all, sin_all = _rope_full_tables()
    pos = np.asarray(positions, dtype=np.int64)
    cos, sin = cos_all[:, pos], sin_all[:, pos]
    C = np.zeros((128, len(positions)), dtype=np.float64)
    S = np.zeros((128, len(positions)), dtype=np.float64)
    for half in range(2):
        r = half * 64
        C[r + 0:r + 64:2] = cos
        C[r + 1:r + 64:2] = cos
        S[r + 0:r + 64:2] = -sin
        S[r + 1:r + 64:2] = sin
    return C.astype(np.float32), S.astype(np.float32)


def _p2_swap():
    P = np.zeros((128, 128), dtype=np.float32)
    for half in range(2):
        r = half * 64
        for j in range(32):
            P[r + 2 * j, r + 2 * j + 1] = 1.0
            P[r + 2 * j + 1, r + 2 * j] = 1.0
    return P.astype(BF)


def host_prep(inputs):
    x = np.asarray(inputs["L0_attn_x"], dtype=np.float32)
    wq = np.asarray(inputs["L0_attn_wq"], dtype=np.float32)
    wk = np.asarray(inputs["L0_attn_wk"], dtype=np.float32)
    wv = np.asarray(inputs["L0_attn_wv"], dtype=np.float32)
    wo = np.asarray(inputs["L0_attn_wo"], dtype=np.float32)
    wg = np.asarray(inputs["L0_ffn_w_gate"], dtype=np.float32)
    wu = np.asarray(inputs["L0_ffn_w_up"], dtype=np.float32)
    wd = np.asarray(inputs["L0_ffn_w_down"], dtype=np.float32)

    d2 = _ff_map()
    sh = {}
    evens = [h for h in range(H) if (h // 4) % 2 == 0]
    odds = [h for h in range(H) if (h // 4) % 2 == 1]
    qcols = []
    for c in range(H // 2):
        qcols += list(range(evens[c] * HD, evens[c] * HD + HD))
        qcols += list(range(odds[c] * HD, odds[c] * HD + HD))
    qcols = np.array(qcols)
    sh["wqT"] = np.ascontiguousarray(wq.T[:, qcols])
    sh["wkT"] = np.ascontiguousarray(wk.T)
    sh["wvT"] = np.ascontiguousarray(wv.T)
    sh["woT"] = np.ascontiguousarray(wo.T[qcols, :])
    for name, w in (("wgS", wg), ("wuS", wu)):
        wt = w.T[:, d2]
        sh[name] = np.ascontiguousarray(
            wt.reshape(DC, 128, FC, 128).transpose(2, 0, 1, 3))
    sh["wdS"] = np.ascontiguousarray(wd.T[d2, :].reshape(FC, 128, D))
    sh["ident"] = np.eye(128, dtype=BF)
    sh["ident32"] = np.eye(128, dtype=np.float32)
    sh["onesf"] = np.ones((1, 128), dtype=np.float32)
    sh["h128"] = _hadamard128()
    sh["p2"] = _p2_swap()

    cores = []
    for core in range(8):
        s, par = core // 2, core % 2
        perm = np.arange(NB)
        if par == 1:
            perm = perm.reshape(-1, 2)[:, ::-1].ravel()
        tok_perm = (perm[:, None] * 128 + np.arange(128)[None, :]).ravel()
        own_tok = tok_perm.reshape(NB, 128)[0::2].ravel()
        d = {}
        d["x_seq"] = np.ascontiguousarray(x[s][tok_perm])
        d["c_q"], d["s_q"] = _rope_tables(own_tok)
        d["c_kp"], d["s_kp"] = _rope_tables(tok_perm)
        mk = np.ones((NB, 128, 128), dtype=np.float32)
        for kb in range(NB):
            i = ilo(kb)
            if kb == 2 * i:
                mk[kb] = np.tril(np.ones((128, 128), np.float32)).T
            elif par == 0:
                mk[kb] = 0.0
        d["maskT"] = mk.astype(BF)
        d["own_tok"] = own_tok
        cores.append(d)
    return sh, cores



class PB:
    def __init__(self, nc):
        self.nc = nc
        self.ops = {e: [] for e in ENGINES}
        self.cnt = {e: 0 for e in ENGINES}
        self.observed = {e: {} for e in ENGINES}
        self.writer = {}
        self.readers = {}
        self.slot_use = {q: [0] * N_SLOTS for q in DMA_QUEUES}
        self.slot_rr = {q: 0 for q in DMA_QUEUES}
        self.sems = {}
        self.out_tokens = []

    def alloc_sems(self, ctx):
        e = ctx.enter_context
        for en in ENGINES:
            self.sems["e_" + en] = e(self.nc.semaphore("e_" + en))
        for q in DMA_QUEUES:
            for s in range(N_SLOTS):
                self.sems[f"d_{q}_{s}"] = e(self.nc.semaphore(f"d_{q}_{s}"))


    def _filter(self, eng, toks):
        best = {}
        for sem, val in toks:
            if val > best.get(sem, 0):
                best[sem] = val
        waits = []
        for sem, val in best.items():
            if self.observed[eng].get(sem, 0) < val:
                waits.append((sem, val))
                self.observed[eng][sem] = val
        return waits

    def _deps(self, reads, writes):
        toks = []
        for b in reads:
            w = self.writer.get(b)
            if w is not None:
                toks.append(w)
        for b in writes:
            w = self.writer.get(b)
            if w is not None:
                toks.append(w)
            toks.extend(self.readers.get(b, ()))
        return toks

    def _update(self, tok, reads, writes):
        for b in writes:
            self.writer[b] = tok
            self.readers[b] = []
        for b in reads:
            if b not in writes:
                self.readers.setdefault(b, []).append(tok)

    def op(self, eng, fn, reads=(), writes=()):
        waits = self._filter(eng, self._deps(reads, writes))
        self.cnt[eng] += 1
        tok = ("e_" + eng, self.cnt[eng])
        self.ops[eng].append((waits, fn, tok))
        self._update(tok, reads, writes)
        return tok

    def dma(self, q, out_ap, in_ap, reads=(), writes=(), is_output=False):
        s = self.slot_rr[q]
        self.slot_rr[q] = (s + 1) % N_SLOTS
        sem = f"d_{q}_{s}"
        uses = self.slot_use[q][s]
        toks = ([(sem, 16 * uses)] if uses else []) + self._deps(reads, writes)
        waits = self._filter(q, toks)
        self.slot_use[q][s] = uses + 1
        tok = (sem, 16 * (uses + 1))

        def fn(eng, out_ap=out_ap, in_ap=in_ap):
            return eng.dma_start(out=out_ap, in_=in_ap)

        self.ops[q].append((waits, fn, tok))
        self._update(tok, reads, writes)
        if is_output:
            self.out_tokens.append(tok)
        return tok

    def barrier(self):
        toks = [("e_" + e2, self.cnt[e2]) for e2 in ENGINES if self.cnt[e2] > 0]
        for q in DMA_QUEUES:
            for s in range(N_SLOTS):
                if self.slot_use[q][s] > 0:
                    toks.append((f"d_{q}_{s}", 16 * self.slot_use[q][s]))
        for e in ENGINES:
            waits = self._filter(e, toks)
            if waits:
                self.ops[e].append((waits, None, None))

    def emit(self, block):
        sems = self.sems
        emitted = {}
        fin = {}
        for sem, val in self.out_tokens:
            q = sem.split("_")[1]
            cur = fin.setdefault(q, {})
            cur[sem] = max(cur.get(sem, 0), val)

        def mk(name):
            def body(eng):
                for waits, fn, tok in self.ops[name]:
                    for wsem, wval in waits:
                        eng.wait_ge(sems[wsem], wval)
                    if fn is None:
                        continue
                    sem, val = tok
                    ins = fn(eng)
                    ins.then_inc(sems[sem], val - emitted.get(sem, 0))
                    emitted[sem] = val
                for sem, val in fin.get(name, {}).items():
                    eng.wait_ge(sems[sem], val)
            return body

        block.tensor(mk("ten"))
        block.vector(mk("vec"))
        block.scalar(mk("sca"))
        block.gpsimd(mk("gps"))
        block.sync(mk("syn"))


XLT = 0
KRO = 32768
VAU = 40960
QRO = 49280
YFM = 65664
PRB = 0
WOT = 0
AFM = 32768
ATM = 49280
X1S = 51328
QXS = 53376
S1O = 0
QXT = 65664
DFM = 65664
RS3 = 82048
BIGN = 83072

PXA = 0
PXB = 2048
PCQ = 0
PSQ = 1024
PXO = 0
PT0 = 2048
PSB = 1024
PG = 3072
PGP = 1024
PN = 4096

STAT = 56
SA, SB = 0, 8
MG, MU, MD = 16, 17, 18
MGH, MUH = 19, 20
MGN, MUN = 21, 22
MDH, MDN = 23, 24
MACC = 25
MPART = 28
MUMD = 29
DQ3 = 32
GCL = 40
LN4 = 48
EXB = 55


def build_nc():
    nc = bass.Bass("TRN2", target_bir_lowering=False)
    g = {}

    def din(name, shape, dt=F32):
        g[name] = nc.dram_tensor(name, shape, dt, kind="ExternalInput").ap()

    din("x_seq", [T, D])
    din("wqT", [D, D])
    din("wkT", [D, HKV * HD])
    din("wvT", [D, HKV * HD])
    din("woT", [D, D])
    din("wgS", [FC, DC, 128, 128])
    din("wuS", [FC, DC, 128, 128])
    din("wdS", [FC, 128, D])
    din("ident", [128, 128], BF16)
    din("ident32", [128, 128])
    din("onesf", [1, 128])
    din("h128", [128, 128], BF16)
    din("p2", [128, 128], BF16)
    din("c_kp", [128, T])
    din("s_kp", [128, T])
    din("c_q", [128, OT])
    din("s_q", [128, OT])
    din("maskT", [NB, 128, 128], BF16)

    g["out"] = nc.dram_tensor("out", [OT, D], F32, kind="ExternalOutput").ap()
    import os as _os
    _dbg = _os.environ.get("KDBG", "") == "1"
    if _os.environ.get("KDBG2", "") == "1":
        g["dbg_qxt"] = nc.dram_tensor("dbg_qxt", [128, 16384], F32, kind="ExternalOutput").ap()
        g["dbg_s1"] = nc.dram_tensor("dbg_s1", [128, 65536], F32, kind="ExternalOutput").ap()
        g["dbg_dfm"] = nc.dram_tensor("dbg_dfm", [128, 16384], F32, kind="ExternalOutput").ap()
    g["x1_park"] = nc.dram_tensor("x1_park", [OT, D], BF16,
                                  kind="ExternalOutput" if _dbg else "Internal").ap()
    g["td_park"] = nc.dram_tensor("td_park", [FC, 128, D], FP8, kind="Internal").ap()

    with contextlib.ExitStack() as ctx:
        e = ctx.enter_context
        bufs = {}
        bufs["big"] = e(nc.sbuf_tensor("big", [128, BIGN], BF16))
        bufs["pool"] = e(nc.sbuf_tensor("pool", [128, PN], F32))
        bufs["u16"] = e(nc.sbuf_tensor("u16", [128, 1024], FP16))
        bufs["c16"] = e(nc.sbuf_tensor("c16", [128, 1024], FP16))
        bufs["wraw"] = e(nc.sbuf_tensor("wraw", [128, 2048], F32))
        bufs["tq"] = e(nc.sbuf_tensor("tq", [128, 2048], BF16))
        bufs["c_id"] = e(nc.sbuf_tensor("c_id", [128, 128], BF16))
        bufs["c_idf"] = e(nc.sbuf_tensor("c_idf", [128, 128], F32))
        bufs["c_of"] = e(nc.sbuf_tensor("c_of", [1, 128], F32))
        bufs["c_h"] = e(nc.sbuf_tensor("c_h", [128, 128], BF16))
        bufs["c_p2"] = e(nc.sbuf_tensor("c_p2", [128, 128], BF16))
        bufs["stat"] = e(nc.sbuf_tensor("stat", [128, STAT], F32))
        bufs["ps_a"] = e(nc.psum_tensor("ps_a", [128, 1024], F32))
        bufs["ps_b"] = e(nc.psum_tensor("ps_b", [128, 1024], F32))
        bufs["ps_c"] = e(nc.psum_tensor("ps_c", [128, 512], F32))
        bufs["ps_d"] = e(nc.psum_tensor("ps_d", [128, 512], F32))
        bufs["pb0"] = e(nc.psum_tensor("pb0", [128, 1024], BF16))
        bufs["pb1"] = e(nc.psum_tensor("pb1", [128, 1024], BF16))

        p = PB(nc)
        p.alloc_sems(ctx)
        with nc.allow_low_precision(reason="bf16 operands with fp32 psum accumulation; tolerance-sanctioned"):
            build_body(nc, p, g, bufs)
            blk = e(nc.Block())
            p.emit(blk)
    return nc


def build_body(nc, p, g, bufs):
    import concourse.bass_isa as bass_isa
    big = bufs["big"]; pool = bufs["pool"]; u16 = bufs["u16"]; c16 = bufs["c16"]
    wraw = bufs["wraw"]; tq = bufs["tq"]
    c_id = bufs["c_id"]; c_idf = bufs["c_idf"]; c_of = bufs["c_of"]
    c_h = bufs["c_h"]; c_p2 = bufs["c_p2"]
    stat = bufs["stat"]
    srow = bufs["pool"][0:1, 1024:2048]
    c_or = bufs["c_h"][0:1, :]
    c_oc = bufs["c_h"][:, 0:1]
    c_msk = bufs["big"][:, 4096:4096 + NB * 128]
    rrow = bufs["big"][0:1, 2048:2560]
    ps_a = bufs["ps_a"]; ps_b = bufs["ps_b"]; ps_c = bufs["ps_c"]
    ps_d = bufs["ps_d"]; pb0 = bufs["pb0"]; pb1 = bufs["pb1"]
    RMAX = bass_isa.ReduceOp.max

    def bg(off, n):
        return big[:, off:off + n]

    def pl(off, n):
        return pool[:, off:off + n]

    def st(c):
        return stat[:, c:c + 1]

    p.dma("syn", c_id[:], g["ident"], writes=["c_id"])
    p.dma("syn", c_idf[:], g["ident32"], writes=["c_idf"])
    p.dma("syn", c_of[:], g["onesf"], writes=["c_of"])
    p.op("vec", lambda v: v.memset(st(EXB), EXPB), writes=["exb"])
    p.dma("syn", c_h[:], g["h128"], writes=["c_h"])
    p.dma("syn", c_p2[:], g["p2"], writes=["c_p2"])


    if KLIM < 0:
        return
    def ln_stats(tile_ap, n, sbase, key_r, junk_ap, junk_key):
        mu, ex2, t2, sr, rs = (st(sbase + i) for i in (0, 1, 2, 3, 4))
        kb = f"st{sbase}_"
        p.op("vec", lambda v: v.reduce_sum(mu, tile_ap, axis=AX.X),
             reads=[key_r], writes=[kb + "mu"])
        p.op("sca", lambda s: s.activation(junk_ap, tile_ap, AF.Square, accum_out=ex2),
             reads=[key_r], writes=[junk_key, kb + "ex"])
        inv = 1.0 / n
        p.op("vec", lambda v: v.tensor_scalar(mu, mu, inv, None, ALU.mult),
             reads=[kb + "mu"], writes=[kb + "mu"])
        p.op("vec", lambda v: v.tensor_scalar(ex2, ex2, inv, None, ALU.mult),
             reads=[kb + "ex"], writes=[kb + "ex"])
        p.op("vec", lambda v: v.tensor_tensor(t2, mu, mu, ALU.mult),
             reads=[kb + "mu"], writes=[kb + "t2"])
        p.op("vec", lambda v: v.tensor_tensor(ex2, ex2, t2, ALU.subtract),
             reads=[kb + "ex", kb + "t2"], writes=[kb + "ex"])
        p.op("vec", lambda v: v.tensor_scalar(ex2, ex2, 1e-5, None, ALU.add),
             reads=[kb + "ex"], writes=[kb + "ex"])
        p.op("sca", lambda s: s.activation(sr, ex2, AF.Sqrt),
             reads=[kb + "ex"], writes=[kb + "sr"])
        p.op("vec", lambda v: v.reciprocal(rs, sr),
             reads=[kb + "sr"], writes=[kb + "rs"])
        p.op("vec", lambda v: v.tensor_tensor(sr, rs, rs, ALU.mult),
             reads=[kb + "rs"], writes=[kb + "sr"])
        p.op("vec", lambda v: v.tensor_tensor(sr, sr, ex2, ALU.mult),
             reads=[kb + "sr", kb + "ex"], writes=[kb + "sr"])
        p.op("vec", lambda v: v.tensor_scalar(sr, sr, -0.5, 1.5, ALU.mult, ALU.add),
             reads=[kb + "sr"], writes=[kb + "sr"])
        p.op("vec", lambda v: v.tensor_tensor(rs, rs, sr, ALU.mult),
             reads=[kb + "rs", kb + "sr"], writes=[kb + "rs"])
        return mu, rs, kb + "mu", kb + "rs"

    def ladder(zq_ap, kzq, out_ap, kout, n):
        u = u16[:, 0:n]
        c = c16[:, 0:n]
        p.op("vec", lambda v: v.tensor_scalar(u, zq_ap, 128.5, 1.0 / 512, ALU.add, ALU.mult),
             reads=[kzq], writes=["lad_u"])
        for _ in range(9):
            p.op("vec", lambda v: v.tensor_scalar(c, u, 0.5, None, ALU.is_ge),
                 reads=["lad_u"], writes=["lad_c"])
            p.op("vec", lambda v: v.scalar_tensor_tensor(u, u, 2.0, c, ALU.mult, ALU.subtract),
                 reads=["lad_u", "lad_c"], writes=["lad_u"])
        p.op("vec", lambda v: v.scalar_tensor_tensor(out_ap, zq_ap, 0.5, u, ALU.add, ALU.subtract),
             reads=[kzq, "lad_u"], writes=[kout])
        p.op("vec", lambda v: v.tensor_scalar(out_ap, out_ap, -128.0, 127.0, ALU.max, ALU.min),
             reads=[kout], writes=[kout])

    xs_d = g["x_seq"].rearrange("(n p) d -> n p d", p=128)
    for i in range(NB):
        par = i % 2
        xa = pl(PXA if par == 0 else PXB, 2048)
        kx = f"poolx{par}"
        p.dma("syn", xa, xs_d[i], writes=[kx])
        xb = tq[:, 0:2048] if par == 0 else bg(YFM, 2048)
        kxb = f"xb{par}"
        mu, rs, kmu, krs = ln_stats(xa, D, SA if par == 0 else SB, kx, xb, kxb)
        p.op("vec", lambda v, xa=xa, xb=xb, mu=mu, rs=rs: v.tensor_scalar(
            xb, xa, mu, rs, ALU.subtract, ALU.mult),
            reads=[kx, kmu, krs], writes=[kxb])
        for dc in range(DC):
            pt = (pb0 if dc % 2 == 0 else pb1)[:, 0:128]
            kp = f"pb{dc % 2}"
            p.op("ten", lambda t, xb=xb, pt=pt, dc=dc: t.transpose(
                pt, xb[:, dc * 128:dc * 128 + 128], c_id[:]),
                reads=[kxb, "c_id"], writes=[kp])
            p.op("sca", lambda s, pt=pt, dst=bg(XLT + dc * 2048 + i * 128, 128):
                 s.copy(dst, pt), reads=[kp], writes=["xlt"])

    if KLIM < 1:
        return
    p.dma("syn", pl(PXA, 2048), g["c_kp"], writes=["poolx0"])
    p.dma("syn", pl(PXB, 2048), g["s_kp"], writes=["poolx1"])

    p.dma("gps", bg(QRO, 8192).rearrange("p (dc v) -> p dc v", v=512),
          g["wkT"].rearrange("(dc p) v -> p dc v", p=128), writes=["wkb"])
    p.dma("gps", bg(QRO + 8192, 8192).rearrange("p (dc v) -> p dc v", v=512),
          g["wvT"].rearrange("(dc p) v -> p dc v", p=128), writes=["wvb"])

    for i in range(NB):
        pm, kpm = (ps_c, "ps_c") if i % 2 == 0 else (ps_d, "ps_d")

        def vmm(t, i=i, pm=pm):
            r = None
            for dc in range(DC):
                r = t.matmul(pm[:, 0:512], bg(XLT + dc * 2048 + i * 128, 128),
                             bg(QRO + 8192 + dc * 512, 512),
                             start=(dc == 0), stop=(dc == DC - 1))
            return r
        p.op("ten", vmm, reads=["xlt", "wvb"], writes=[kpm])
        vdst = bg(VAU + i * 520, 520).rearrange("p (h v) -> p h v", v=65)
        p.op("vec", lambda v, pm=pm, vdst=vdst: v.tensor_copy(
            vdst[:, :, 0:64], pm[:, 0:512].rearrange("p (h v) -> p h v", v=64)),
            reads=[kpm], writes=["vau"])
        p.op("vec", lambda v, vdst=vdst: v.memset(vdst[:, :, 64:65], 1.0),
             writes=["vau"])

    for kc in range(4):
        for tc in range(4):
            pm, kpm = (ps_c, "ps_c") if tc % 2 == 0 else (ps_d, "ps_d")

            def kmm(t, kc=kc, tc=tc, pm=pm):
                r = None
                for dc in range(DC):
                    r = t.matmul(pm[:, 0:512], bg(QRO + dc * 512 + kc * 128, 128),
                                 bg(XLT + dc * 2048 + tc * 512, 512),
                                 start=(dc == 0), stop=(dc == DC - 1))
                return r
            p.op("ten", kmm, reads=["xlt", "wkb"], writes=[kpm])
            kraw = tq[:, 0:512] if tc % 2 == 0 else tq[:, 512:1024]
            kkr = f"tqk{tc % 2}"
            p.op("sca", lambda s, pm=pm, kraw=kraw: s.copy(kraw, pm[:, 0:512]),
                 reads=[kpm], writes=[kkr])
            p.op("ten", lambda t, kraw=kraw, pm=pm: t.matmul(
                pm[:, 0:512], c_p2[:], kraw, start=True, stop=True),
                reads=[kkr, "c_p2", kpm], writes=[kpm])
            t1 = u16[:, 0:512]
            t2 = c16[:, 0:512]
            p.op("vec", lambda v, kraw=kraw, tc=tc, t1=t1: v.tensor_tensor(
                t1, kraw, pl(PXA + tc * 512, 512), ALU.mult),
                reads=[kkr, "poolx0"], writes=["ropet1"])
            p.op("vec", lambda v, pm=pm, tc=tc, t2=t2: v.tensor_tensor(
                t2, pm[:, 0:512], pl(PXB + tc * 512, 512), ALU.mult),
                reads=[kpm, "poolx1"], writes=["ropet2"])
            p.op("vec", lambda v, kc=kc, tc=tc, t1=t1, t2=t2: v.tensor_tensor(
                bg(KRO + kc * 2048 + tc * 512, 512), t1, t2, ALU.add),
                reads=["ropet1", "ropet2"], writes=["kro"])

    if KLIM < 2:
        return
    p.barrier()

    KSUB = int(os.environ.get("KSUB", "99"))
    p.dma("syn", pl(PCQ, 1024), g["c_q"], reads=["poolx0"], writes=["poolcq"])
    p.dma("syn", pl(PSQ, 1024), g["s_q"], reads=["poolx0", "poolx1"], writes=["poolsq"])
    for og in range(4):
        wqg = bg(YFM, 8192)
        p.dma("gps", wqg.rearrange("p (dc v) -> p dc v", v=512),
              g["wqT"][:, og * 512:og * 512 + 512].rearrange(
                  "(dc p) v -> p dc v", p=128), writes=["wqg"])
        for oc4 in range(4):
            oc = og * 4 + oc4
            for tc in range(2):
                pm, kpm = (ps_c, "ps_c") if (oc4 * 2 + tc) % 2 == 0 else (ps_d, "ps_d")

                def qmm(t, oc4=oc4, tc=tc, pm=pm):
                    r = None
                    for dc in range(DC):
                        mv = bg(XLT + dc * 2048 + tc * 1024, 1024).rearrange(
                            "p (b t) -> p b t", t=128)[:, 0:8:2, :]
                        r = t.matmul(pm[:, 0:512],
                                     bg(YFM + dc * 512 + oc4 * 128, 128), mv,
                                     start=(dc == 0), stop=(dc == DC - 1))
                    return r
                if KSUB < 1:
                    continue
                p.op("ten", qmm, reads=["xlt", "wqg"], writes=[kpm])
                if KSUB < 2:
                    continue
                qraw = tq[:, 0:512] if tc == 0 else tq[:, 512:1024]
                kqr = f"tqk{tc}"
                p.op("sca", lambda s, pm=pm, qraw=qraw: s.copy(qraw, pm[:, 0:512]),
                     reads=[kpm], writes=[kqr])
                if KSUB < 3:
                    continue
                p.op("ten", lambda t, qraw=qraw, pm=pm: t.matmul(
                    pm[:, 0:512], c_p2[:], qraw, start=True, stop=True),
                    reads=[kqr, "c_p2", kpm], writes=[kpm])
                t1 = u16[:, 0:512]
                t2 = c16[:, 0:512]
                p.op("vec", lambda v, qraw=qraw, tc=tc, t1=t1: v.tensor_tensor(
                    t1, qraw, pl(PCQ + tc * 512, 512), ALU.mult),
                    reads=[kqr, "poolcq"], writes=["ropet1"])
                p.op("vec", lambda v, pm=pm, tc=tc, t2=t2: v.tensor_tensor(
                    t2, pm[:, 0:512], pl(PSQ + tc * 512, 512), ALU.mult),
                    reads=[kpm, "poolsq"], writes=["ropet2"])
                p.op("vec", lambda v, oc=oc, tc=tc, t1=t1, t2=t2: v.tensor_tensor(
                    bg(QRO + oc * 1024 + tc * 512, 512), t1, t2, ALU.add),
                    reads=["ropet1", "ropet2"], writes=["qro"])
        p.barrier()

    if KLIM < 3:
        return
    for wname, dram, acc_c, m_c, eng, sl in (
            ("wg", g["wgS"], MACC, MG, "vec", 0),
            ("wu", g["wuS"], MACC + 1, MU, "vec", 1),
            ("wd", g["wdS"], MACC + 2, MD, "sca", 2)):
        flat = (dram.rearrange("a b c -> (a b c)") if wname == "wd"
                else dram.rearrange("a b c d -> (a b c d)"))
        tiles = flat.rearrange("(n p m) -> n p m", p=128, m=1024)
        acc = st(acc_c)
        kacc = f"macc_{wname}"
        p.op("vec", lambda v, acc=acc: v.memset(acc, 0.0), writes=[kacc])
        part = st(MPART + sl)
        kpart = f"mpart_{wname}"
        for i in range(128):
            wslot = wraw[:, (i % 2) * 1024:(i % 2) * 1024 + 1024]
            kw = f"wraw{(i % 2)}"
            p.dma("syn", wslot, tiles[i], writes=[kw])
            if eng == "sca":
                p.op("sca", lambda s, wslot=wslot, part=part: s.activation(
                    tq[:, 0:1024], wslot, AF.Abs, accum_out=part),
                    reads=[kw], writes=["tgu", kpart])
            else:
                p.op(eng, lambda v, wslot=wslot, part=part: v.reduce_sum(
                    part, wslot, axis=AX.X, apply_absolute_value=True),
                    reads=[kw], writes=[kpart])
            p.op("vec", lambda v, acc=acc, part=part: v.tensor_tensor(
                acc, acc, part, ALU.add), reads=[kacc, kpart], writes=[kacc])
        mcol = st(m_c)
        m1 = pool[0:1, 1536:1537]
        p.op("ten", lambda t, acc=acc: t.transpose(
            ps_c[0:1, 0:128], acc, c_idf[:]), reads=[kacc, "c_idf"], writes=["ps_c"])
        p.op("vec", lambda v, m1=m1: v.reduce_sum(m1, ps_c[0:1, 0:128], axis=AX.X),
             reads=["ps_c"], writes=["m1s"])
        p.op("vec", lambda v, m1=m1: v.tensor_scalar(
            m1, m1, 1.0 / 16777216.0, 1e-5, ALU.mult, ALU.max),
            reads=["m1s"], writes=["m1s"])
        p.op("ten", lambda t, m1=m1: t.matmul(
            ps_c[:, 0:1], c_of[0:1, :], m1, start=True, stop=True),
            reads=["m1s", "c_of"], writes=["ps_c"])
        p.op("vec", lambda v, mcol=mcol: v.tensor_copy(mcol, ps_c[:, 0:1]),
             reads=["ps_c"], writes=[f"m_{wname}"])
    for src, dsth, dstn in ((MG, MGH, MGN), (MU, MUH, MUN), (MD, MDH, MDN)):
        p.op("vec", lambda v, src=src, dsth=dsth: v.tensor_scalar(
            st(dsth), st(src), 0.5, None, ALU.mult),
            reads=[f"m_{'wg' if src == MG else 'wu' if src == MU else 'wd'}"],
            writes=[f"mh{dsth}"])
        p.op("vec", lambda v, src=src, dstn=dstn: v.tensor_scalar(
            st(dstn), st(src), -0.5, None, ALU.mult),
            reads=[f"m_{'wg' if src == MG else 'wu' if src == MU else 'wd'}"],
            writes=[f"mh{dstn}"])
    p.op("vec", lambda v: v.tensor_tensor(st(MUMD), st(MU), st(MD), ALU.mult),
         reads=["m_wu", "m_wd"], writes=["mumd"])
    p.op("vec", lambda v: v.tensor_scalar(
        st(MUMD), st(MUMD), 1.0 / math.sqrt(8192.0), None, ALU.mult),
        reads=["mumd"], writes=["mumd"])

    if KLIM < 4:
        return
    p.dma("syn", c_msk.rearrange("p (n q) -> p n q", q=128),
          g["maskT"].rearrange("n p q -> p n q"), writes=["c_msk"])

    evens = [h for h in range(H) if (h // 4) % 2 == 0]
    odds = [h for h in range(H) if (h // 4) % 2 == 1]
    qpos = {}
    for c in range(H // 2):
        qpos[evens[c]] = (c, 0)
        qpos[odds[c]] = (c, 1)
    for h in range(H):
        kvh = h // 4
        kc, kb64 = kvh // 2, (kvh % 2) * 64
        qc, qhalf = qpos[h]
        qb64 = qhalf * 64
        kav0, kav1 = "avc", "avd"
        for kb in range(NB):
            q0 = ilo(kb) * 128
            w = OT - q0
            sT, ksT = (ps_a, "ps_a") if (h * NB + kb) % 2 == 0 else (ps_b, "ps_b")

            def smm(t, kc=kc, kb64=kb64, kb=kb, qc=qc, qb64=qb64, q0=q0, w=w, sT=sT):
                r = None
                for j in range((w + 511) // 512):
                    n = min(512, w - j * 512)
                    r = t.matmul(
                        sT[:, j * 512:j * 512 + n],
                        big[kb64:kb64 + 64, KRO + kc * 2048 + kb * 128:
                            KRO + kc * 2048 + kb * 128 + 128],
                        big[qb64:qb64 + 64, QRO + qc * 1024 + q0 + j * 512:
                            QRO + qc * 1024 + q0 + j * 512 + n],
                        start=True, stop=True)
                return r
            p.op("ten", smm, reads=["kro", "qro"], writes=[ksT])
            prb = bg(PRB + (kb % 2) * 1024, 1024)
            kprb = f"prb{kb % 2}"
            p.op("sca", lambda s, sT=sT, prb=prb, w=w: s.activation(
                prb[:, 0:w], sT[:, 0:w], AF.Exp, bias=st(EXB), scale=0.125),
                reads=[ksT, "exb"], writes=[kprb])
            p.op("vec", lambda v, prb=prb, kb=kb: v.tensor_tensor(
                prb[:, 0:128], prb[:, 0:128],
                c_msk[:, kb * 128:kb * 128 + 128], ALU.mult),
                reads=[kprb, "c_msk"], writes=[kprb])

            def avmm(t, kb=kb, kvh=kvh, q0=q0, prb=prb):
                r = None
                for c in (0, 1):
                    lo = max(q0, c * 512)
                    if lo >= (c + 1) * 512:
                        continue
                    pmv = ps_c if c == 0 else ps_d
                    stop = (kb == NB - 1) if c == 1 else (kb == 7)
                    r = t.matmul(
                        pmv[0:65, lo - c * 512:512],
                        bg(VAU + kb * 520 + kvh * 65, 65),
                        prb[:, lo - q0:lo - q0 + 512 - (lo - c * 512)],
                        start=(kb == 0), stop=stop)
                return r
            p.op("ten", avmm, reads=[kprb, "vau"],
                 writes=([kav0] if ilo(kb) < 4 else []) + [kav1])

        for c in (0, 1):
            pmv, kav = (ps_c, "avc") if c == 0 else (ps_d, "avd")
            rrow2 = rrow[0:1, 0:512]
            p.op("vec", lambda v, pmv=pmv, rrow2=rrow2: v.reciprocal(
                rrow2, pmv[64:65, 0:512]), reads=[kav], writes=["rrow"])
            p.op("ten", lambda t, rrow2=rrow2: t.matmul(
                ps_a[0:64, 0:512], c_or[0:1, 0:64], rrow2, start=True, stop=True),
                reads=["rrow", "c_h"], writes=["ps_a"])
            p.op("sca", lambda s: s.copy(pool[0:64, 0:512], ps_a[0:64, 0:512]),
                 reads=["ps_a"], writes=["rbc"])
            p.op("vec", lambda v, pmv=pmv, h=h, qc=qc, qb64=qb64, c=c: v.tensor_tensor(
                big[qb64:qb64 + 64, YFM + qc * 1024 + c * 512:
                    YFM + qc * 1024 + c * 512 + 512],
                pmv[0:64, 0:512], pool[0:64, 0:512], ALU.mult),
                reads=[kav, "rbc"], writes=["yfm"])

    if KLIM < 5:
        return
    p.barrier()

    for part4 in range(4):
        p.dma("gps", bg(WOT + part4 * 8192, 8192).rearrange(
                  "p (dc v) -> p dc v", v=2048),
              g["woT"][part4 * 512:part4 * 512 + 512].rearrange(
                  "(dc p) v -> p dc v", p=128), writes=[f"wot{part4}"])
    for oc in range(DC):
        for tc in range(2):
            pm, kpm = (ps_c, "ps_c") if (oc * 2 + tc) % 2 == 0 else (ps_d, "ps_d")

            def omm(t, oc=oc, tc=tc, pm=pm):
                r = None
                for dc in range(DC):
                    r = t.matmul(pm[:, 0:512],
                                 bg(WOT + dc * 2048 + oc * 128, 128),
                                 bg(YFM + dc * 1024 + tc * 512, 512),
                                 start=(dc == 0), stop=(dc == DC - 1))
                return r
            p.op("ten", omm, reads=["wot0", "wot1", "wot2", "wot3", "yfm"],
                 writes=[kpm])
            p.op("sca", lambda s, pm=pm, oc=oc, tc=tc: s.copy(
                bg(AFM + oc * 1024 + tc * 512, 512), pm[:, 0:512]),
                reads=[kpm], writes=["afm"])

    p.barrier()

    x1r = g["x1_park"].rearrange("(n p) d -> n p d", p=128)
    outr = g["out"].rearrange("(n p) d -> n p d", p=128)
    for i in range(OB):
        for dc in range(DC):
            pt = (pb0 if dc % 2 == 0 else pb1)[:, 0:128]
            kp = f"pb{dc % 2}"
            p.op("ten", lambda t, dc=dc, i=i, pt=pt: t.transpose(
                pt, bg(AFM + dc * 1024 + i * 128, 128), c_id[:]),
                reads=["afm", "c_id"], writes=[kp])
            p.op("sca", lambda s, pt=pt, dc=dc: s.copy(bg(ATM + dc * 128, 128), pt),
                 reads=[kp], writes=["atm"])
        mu2, rs2, kmu2, krs2 = ln_stats(bg(ATM, 2048), D, SA, "atm",
                                        bg(X1S, 2048), "x1s")
        xo = pl(PXO, 2048)
        p.dma("syn", xo, g["x_seq"][i * 256:i * 256 + 128, :], writes=["xo"])
        p.op("vec", lambda v, mu2=mu2, rs2=rs2: v.tensor_scalar(
            bg(X1S, 2048), bg(ATM, 2048), mu2, rs2, ALU.subtract, ALU.mult),
            reads=["atm", kmu2, krs2], writes=["x1s"])
        p.op("vec", lambda v, xo=xo: v.tensor_tensor(
            bg(X1S, 2048), bg(X1S, 2048), xo, ALU.add),
            reads=["x1s", "xo"], writes=["x1s"])
        p.dma("syn", x1r[i], bg(X1S, 2048), reads=["x1s"], writes=[f"x1p{i}"])
        mu3, rs3, kmu3, krs3 = ln_stats(bg(X1S, 2048), D, SB, "x1s",
                                        bg(QXS, 2048), "qxs")
        mxp, mxn = st(SA + 5), st(SA + 6)
        p.op("vec", lambda v, mxp=mxp: v.reduce_max(mxp, bg(X1S, 2048), axis=AX.X),
             reads=["x1s"], writes=["mxp"])
        p.op("vec", lambda v, mxn=mxn: v.tensor_reduce(
            mxn, bg(X1S, 2048), axis=AX.X, op=ALU.min),
            reads=["x1s"], writes=["mxn"])
        p.op("vec", lambda v, mxp=mxp, mu3=mu3: v.tensor_tensor(
            mxp, mxp, mu3, ALU.subtract), reads=["mxp", kmu3], writes=["mxp"])
        p.op("vec", lambda v, mxn=mxn, mu3=mu3: v.tensor_tensor(
            mxn, mu3, mxn, ALU.subtract), reads=["mxn", kmu3], writes=["mxn"])
        p.op("vec", lambda v, mxp=mxp, mxn=mxn: v.tensor_tensor(
            mxp, mxp, mxn, ALU.max), reads=["mxp", "mxn"], writes=["mxp"])
        p.op("vec", lambda v, mxp=mxp, rs3=rs3: v.tensor_tensor(
            mxp, mxp, rs3, ALU.mult), reads=["mxp", krs3], writes=["mxp"])
        p.op("vec", lambda v, mxp=mxp: v.tensor_scalar(
            mxp, mxp, 1e-5, None, ALU.max), reads=["mxp"], writes=["mxp"])
        dq3 = st(DQ3 + i)
        p.op("vec", lambda v, mxp=mxp, dq3=dq3: v.tensor_scalar(
            dq3, mxp, 1.0 / 128.0, None, ALU.mult),
            reads=["mxp"], writes=[f"dq3_{i}"])
        s3 = st(SA + 7)
        p.op("vec", lambda v, mxp=mxp, s3=s3: v.reciprocal(s3, mxp),
             reads=["mxp"], writes=["s3"])
        p.op("vec", lambda v, s3=s3: v.tensor_scalar(s3, s3, 128.0, None, ALU.mult),
             reads=["s3"], writes=["s3"])
        rcq = st(SA + 5)
        p.op("vec", lambda v, rcq=rcq, rs3=rs3, s3=s3: v.tensor_tensor(
            rcq, rs3, s3, ALU.mult), reads=[krs3, "s3"], writes=["rcq"])
        zq = pl(PT0, 2048)
        p.op("vec", lambda v, zq=zq, mu3=mu3, rcq=rcq: v.tensor_scalar(
            zq, bg(X1S, 2048), mu3, rcq, ALU.subtract, ALU.mult),
            reads=["x1s", kmu3, "rcq"], writes=["zq"])
        ladder(zq[:, 0:1024], "zq", bg(QXS, 1024), "qxs", 1024)
        ladder(zq[:, 1024:2048], "zq", bg(QXS + 1024, 1024), "qxs", 1024)
        for dc in range(DC):
            pt = (pb0 if dc % 2 == 0 else pb1)[:, 0:128]
            kp = f"pb{dc % 2}"
            p.op("ten", lambda t, dc=dc, pt=pt: t.transpose(
                pt, bg(QXS + dc * 128, 128), c_id[:]),
                reads=["qxs", "c_id"], writes=[kp])
            p.op("sca", lambda s, pt=pt, dc=dc, i=i: s.copy(
                bg(QXT + dc * 1024 + i * 128, 128), pt),
                reads=[kp], writes=["qxt"])

    for i in range(OB):
        p.op("ten", lambda t, i=i: t.transpose(
            ps_c[0:1, 0:128], st(DQ3 + i), c_idf[:]),
            reads=[f"dq3_{i}", "c_idf"], writes=["ps_c"])
        p.op("vec", lambda v, i=i: v.tensor_copy(srow[0:1, i * 128:i * 128 + 128],
                                                 ps_c[0:1, 0:128]),
             reads=["ps_c"], writes=["srowd"])
    p.barrier()

    if KLIM < 6:
        return
    if "dbg_qxt" in g:
        p.dma("gps", g["dbg_qxt"], bg(QXT, 16384), reads=["qxt"], is_output=True)

    for tc in range(2):
        pm = ps_c if tc == 0 else ps_d

        def rmm(t, tc=tc, pm=pm):
            r = None
            for dc in range(DC):
                r = t.matmul(pm[0:1, 0:512], c_oc,
                             bg(QXT + dc * 1024 + tc * 512, 512),
                             start=(dc == 0), stop=(dc == DC - 1))
            return r
        p.op("ten", rmm, reads=["qxt", "c_h"], writes=["ps_c" if tc == 0 else "ps_d"])
        p.op("vec", lambda v, pm=pm, tc=tc: v.tensor_scalar(
            big[0:1, RS3 + tc * 512:RS3 + tc * 512 + 512], pm[0:1, 0:512],
            -1.0, None, ALU.mult),
            reads=["ps_c" if tc == 0 else "ps_d"], writes=["rs3row"])
    p.op("vec", lambda v: v.tensor_copy(big[0:1, 0:1024], srow[0:1, 0:1024]),
         reads=["srowd"], writes=["dq3bf"])
    for half in (0, 1):
        p.op("ten", lambda t, half=half: t.matmul(
            ps_c[:, 0:512], c_or[0:1, :],
            big[0:1, half * 512:half * 512 + 512], start=True, stop=True),
            reads=["dq3bf", "c_h"], writes=["ps_c"])
        p.op("vec", lambda v, half=half: v.tensor_copy(
            pl(PXA + half * 512, 512), ps_c[:, 0:512]),
            reads=["ps_c"], writes=["dqb"])

    for fc in range(FC):
        for wname, dram, mh, mn, pm, kpm in (
                ("g", g["wgS"], MGH, MGN, ps_a, "ps_a"),
                ("u", g["wuS"], MUH, MUN, ps_b, "ps_b")):
            if wname == "g" or fc >= 62:
                tgu, ktgu = tq[:, 0:2048], "tgu"
            else:
                tgu, ktgu = bg(S1O + 62 * 1024, 2048), "tgu1"
            for hf in range(2):
                wslot = wraw[:, hf * 1024:hf * 1024 + 1024]
                kw = f"wraw{hf}"
                p.dma("syn", wslot.rearrange("p (dc f) -> p dc f", f=128),
                      dram[fc, hf * 8:hf * 8 + 8].rearrange("dc p f -> p dc f"),
                      writes=[kw])
                ca = u16[:, 0:1024]
                cb = c16[:, 0:1024]
                p.op("gps", lambda gq, wslot=wslot, ca=ca, mh=mh: gq.tensor_scalar(
                    ca, wslot, st(mh), None, ALU.is_ge),
                    reads=[kw, f"mh{mh}"], writes=["qc1"])
                p.op("vec", lambda v, wslot=wslot, cb=cb, mn=mn: v.tensor_scalar(
                    cb, wslot, st(mn), None, ALU.is_ge),
                    reads=[kw, f"mh{mn}"], writes=["qc2"])
                p.op("vec", lambda v, ca=ca, cb=cb, hf=hf, tgu=tgu: v.tensor_tensor(
                    tgu[:, hf * 1024:hf * 1024 + 1024], ca, cb, ALU.add),
                    reads=["qc1", "qc2"], writes=[ktgu])
            for tc in range(2):
                def gmm(t, tc=tc, pm=pm, tgu=tgu):
                    r = None
                    for dc in range(DC):
                        r = t.matmul(pm[:, tc * 512:tc * 512 + 512],
                                     tgu[:, dc * 128:dc * 128 + 128],
                                     bg(QXT + dc * 1024 + tc * 512, 512),
                                     start=(dc == 0), stop=False)
                    r = t.matmul(pm[:, tc * 512:tc * 512 + 512], c_or[0:1, 0:128],
                                 big[0:1, RS3 + tc * 512:RS3 + tc * 512 + 512],
                                 start=False, stop=True)
                    return r
                p.op("ten", gmm, reads=[ktgu, "qxt", "rs3row", "c_h"], writes=[kpm])
        for tc in range(2):
            gt0 = pl(PXB + tc * 512, 512)
            gt1 = pl(PXB + 1024 + tc * 512, 512)
            p.op("vec", lambda v, tc=tc, gt0=gt0: v.tensor_tensor(
                gt0, ps_a[:, tc * 512:tc * 512 + 512], pl(PXA + tc * 512, 512),
                ALU.mult), reads=["ps_a", "dqb"], writes=["gt0"])
            p.op("sca", lambda s, gt0=gt0, gt1=gt1: s.activation(
                gt1, gt0, AF.Silu, scale=st(MG)),
                reads=["gt0", "m_wg"], writes=["gt1"])
            itile = tq[:, 0:512]
            p.op("vec", lambda v, tc=tc, gt1=gt1, itile=itile: v.tensor_tensor(
                itile, gt1, ps_b[:, tc * 512:tc * 512 + 512], ALU.mult),
                reads=["gt1", "ps_b"], writes=["int", "tgu"])
            pm, kpm = (ps_c, "ps_c") if tc == 0 else (ps_d, "ps_d")
            p.op("ten", lambda t, itile=itile, pm=pm: t.matmul(
                pm[:, 0:512], c_h[:], itile, start=True, stop=True),
                reads=["int", "tgu", "c_h", kpm], writes=[kpm])
            p.op("sca", lambda s, pm=pm, fc=fc, tc=tc: s.copy(
                bg(S1O + fc * 1024 + tc * 512, 512), pm[:, 0:512]),
                reads=[kpm], writes=["s1"] + (["tgu1"] if fc >= 62 else []))

    p.barrier()

    if KLIM < 7:
        return
    s1all = bg(S1O, 65536)
    for grp in range(32):
        w0 = grp * 32
        for b in range(6):
            period, half = 2 << b, 1 << b
            nhi = 64 // period
            hi_split = (period * 1024) > 16384
            for hi in (range(nhi) if hi_split else (None,)):
                if hi is None:
                    sview = s1all.rearrange("p (hi sub t) -> p hi sub t",
                                            sub=period, t=1024)
                    wview = tq[:, 0:2048].rearrange("p (hi sub t) -> p hi sub t",
                                                    sub=period, t=32)
                    s_lo = sview[:, :, 0:half, w0:w0 + 32]
                    s_hi = sview[:, :, half:period, w0:w0 + 32]
                    w_lo = wview[:, :, 0:half, :]
                    w_hi = wview[:, :, half:period, :]
                else:
                    sbase = bg(S1O + hi * period * 1024, period * 1024).rearrange(
                        "p (sub t) -> p sub t", t=1024)
                    wbase = tq[:, hi * period * 32:(hi + 1) * period * 32].rearrange(
                        "p (sub t) -> p sub t", t=32)
                    s_lo = sbase[:, 0:half, w0:w0 + 32]
                    s_hi = sbase[:, half:period, w0:w0 + 32]
                    w_lo = wbase[:, 0:half, :]
                    w_hi = wbase[:, half:period, :]
                if b % 2 == 0:
                    a_lo, a_hi, d_lo, d_hi, rk, wk2 = s_lo, s_hi, w_lo, w_hi, "s1", "bflyW"
                else:
                    a_lo, a_hi, d_lo, d_hi, rk, wk2 = w_lo, w_hi, s_lo, s_hi, "bflyW", "s1"
                p.op("vec", lambda v, a=a_lo, bb=a_hi, o=d_lo: v.tensor_tensor(
                    o, a, bb, ALU.add), reads=[rk], writes=[wk2])
                p.op("vec", lambda v, a=a_lo, bb=a_hi, o=d_hi: v.tensor_tensor(
                    o, a, bb, ALU.subtract), reads=[rk], writes=[wk2])
        srcr = s1all.rearrange("p (c t) -> p t c", t=1024)[:, w0:w0 + 32, :]
        p.op("vec", lambda v, srcr=srcr, w0=w0: v.tensor_reduce(
            pl(PXA + w0, 32), srcr, axis=AX.X,
            op=ALU.max, apply_absolute_value=True),
            reads=["s1"], writes=["acc"])
    for i in range(OB):
        p.op("ten", lambda t, i=i: t.transpose(
            ps_c[:, 0:128], pl(PXA, 1024)[:, i * 128:i * 128 + 128], c_idf[:]),
            reads=["acc", "c_idf"], writes=["ps_c"])
        mx, dq2, sc = st(SA), st(SA + 1), st(SA + 2)
        p.op("vec", lambda v, mx=mx: v.reduce_max(
            mx, ps_c[:, 0:128], axis=AX.X, apply_absolute_value=True),
            reads=["ps_c"], writes=["f2mx"])
        p.op("vec", lambda v, mx=mx: v.tensor_scalar(mx, mx, 1e-5, None, ALU.max),
             reads=["f2mx"], writes=["f2mx"])
        p.op("vec", lambda v, mx=mx, dq2=dq2: v.tensor_scalar(
            dq2, mx, 1.0 / 128.0, None, ALU.mult), reads=["f2mx"], writes=["f2dq2"])
        p.op("vec", lambda v, dq2=dq2, i=i: v.tensor_tensor(
            st(GCL + i), dq2, st(DQ3 + i), ALU.mult),
            reads=["f2dq2", f"dq3_{i}"], writes=[f"gc_{i}"])
        p.op("vec", lambda v, i=i: v.tensor_tensor(
            st(GCL + i), st(GCL + i), st(MUMD), ALU.mult),
            reads=[f"gc_{i}", "mumd"], writes=[f"gc_{i}"])
        p.op("vec", lambda v, mx=mx, sc=sc: v.reciprocal(sc, mx),
             reads=["f2mx"], writes=["f2sc"])
        p.op("vec", lambda v, sc=sc: v.tensor_scalar(sc, sc, 128.0, None, ALU.mult),
             reads=["f2sc"], writes=["f2sc"])
        p.op("ten", lambda t, sc=sc: t.transpose(
            ps_d[0:1, 0:128], sc, c_idf[:]),
            reads=["f2sc", "c_idf"], writes=["ps_d"])
        p.op("vec", lambda v, i=i: v.tensor_copy(
            srow[0:1, i * 128:i * 128 + 128], ps_d[0:1, 0:128]),
            reads=["ps_d"], writes=["f2srow"])
    p.op("vec", lambda v: v.tensor_copy(
        big[0:1, RS3:RS3 + 1024], srow[0:1, 0:1024]),
        reads=["f2srow"], writes=["srowbf"])
    for half in (0, 1):
        p.op("ten", lambda t, half=half: t.matmul(
            ps_c[:, 0:512], c_or[0:1, :],
            big[0:1, RS3 + half * 512:RS3 + half * 512 + 512],
            start=True, stop=True), reads=["srowbf", "c_h"], writes=["ps_c"])
        p.op("vec", lambda v, half=half: v.tensor_copy(
            pl(PSB + half * 512, 512), ps_c[:, 0:512]),
            reads=["ps_c"], writes=["sbc"])

    for grp in range(64):
        w0 = grp * 16
        src3 = s1all.rearrange("p (c t) -> p c t", t=1024)[:, :, w0:w0 + 16]
        zq = pl(PXB, 1024).rearrange("p (c t) -> p c t", t=16)
        sbw = pl(PSB, 1024)[:, w0:w0 + 16].unsqueeze(1).broadcast_to([128, 64, 16])
        p.op("vec", lambda v, src3=src3, zq=zq, sbw=sbw: v.tensor_tensor(
            zq, src3, sbw, ALU.mult), reads=["s1", "sbc"], writes=["zqf"])
        zqf = pl(PXB, 1024)
        ladder(zqf, "zqf", tq[:, 0:1024], "qf", 1024)
        qv = tq[:, 0:1024].rearrange("p (c t) -> p c t", t=16)
        p.op("vec", lambda v, qv=qv, src3=src3: v.tensor_copy(src3, qv),
             reads=["qf"], writes=["s1"])

    for tc in range(2):
        pm, kpm = (ps_c, "ps_c") if tc == 0 else (ps_d, "ps_d")

        def rfm(t, tc=tc, pm=pm):
            r = None
            for mc in range(FC):
                r = t.matmul(pm[0:1, 0:512], c_oc,
                             bg(S1O + mc * 1024 + tc * 512, 512),
                             start=(mc == 0), stop=(mc == FC - 1))
            return r
        p.op("ten", rfm, reads=["s1", "c_h"], writes=[kpm])
        p.op("vec", lambda v, pm=pm, tc=tc: v.tensor_scalar(
            big[0:1, RS3 + tc * 512:RS3 + tc * 512 + 512], pm[0:1, 0:512],
            -1.0, None, ALU.mult),
            reads=[kpm], writes=["rsfrow"])

    if "dbg_s1" in g:
        for qq in range(4):
            p.dma("gps", g["dbg_s1"][:, qq * 16384:(qq + 1) * 16384],
                  bg(S1O + qq * 16384, 16384), reads=["s1"], is_output=True)
    p.barrier()

    if KLIM < 8:
        return
    ocgs = [(0, 3), (3, 3), (6, 3), (9, 3), (12, 2), (14, 2)]
    for gi, (oc0, no) in enumerate(ocgs):
        for mc in range(FC):
            if gi == 0:
                td = tq[:, 0:2048]
                ktd = "td"
            else:
                td = tq[:, (mc % 2) * 1024:(mc % 2) * 1024 + 1024]
                ktd = f"td{mc % 2}"
            if gi == 0:
                for hf in range(2):
                    wslot = wraw[:, hf * 1024:hf * 1024 + 1024]
                    kw = f"wraw{hf}"
                    p.dma("syn", wslot, g["wdS"][mc][:, hf * 1024:hf * 1024 + 1024],
                          writes=[kw])
                    ca, cb = u16[:, 0:1024], c16[:, 0:1024]
                    p.op("gps", lambda gq, wslot=wslot, ca=ca: gq.tensor_scalar(
                        ca, wslot, st(MDH), None, ALU.is_ge),
                        reads=[kw, f"mh{MDH}"], writes=["qc1"])
                    p.op("vec", lambda v, wslot=wslot, cb=cb: v.tensor_scalar(
                        cb, wslot, st(MDN), None, ALU.is_ge),
                        reads=[kw, f"mh{MDN}"], writes=["qc2"])
                    p.op("vec", lambda v, ca=ca, cb=cb, td=td, hf=hf: v.tensor_tensor(
                        td[:, hf * 1024:hf * 1024 + 1024], ca, cb, ALU.add),
                        reads=["qc1", "qc2"], writes=[ktd])
                p.dma("gps", g["td_park"][mc], td, reads=[ktd],
                      writes=[f"tdp{mc}"])
            else:
                p.dma("gps", td[:, 0:no * 128],
                      g["td_park"][mc][:, oc0 * 128:(oc0 + no) * 128],
                      reads=[f"tdp{mc}"], writes=[ktd, "td"])

            def dmm(t, gi=gi, oc0=oc0, no=no, mc=mc, td=td):
                r = None
                for j in range(no):
                    off = (oc0 + j) * 128 if gi == 0 else j * 128
                    for tc in range(2):
                        pmv = [ps_a[:, 0:512], ps_a[:, 512:1024],
                               ps_b[:, 0:512], ps_b[:, 512:1024],
                               ps_c[:, 0:512], ps_d[:, 0:512]][j * 2 + tc]
                        r = t.matmul(pmv, td[:, off:off + 128],
                                     bg(S1O + mc * 1024 + tc * 512, 512),
                                     start=(mc == 0), stop=False)
                return r
            p.op("ten", dmm, reads=[ktd, "s1"], writes=["psD"])
        def daug(t, no=no):
            r = None
            for j in range(no):
                for tc in range(2):
                    pmv = [ps_a[:, 0:512], ps_a[:, 512:1024],
                           ps_b[:, 0:512], ps_b[:, 512:1024],
                           ps_c[:, 0:512], ps_d[:, 0:512]][j * 2 + tc]
                    r = t.matmul(pmv, c_or[0:1, 0:128],
                                 big[0:1, RS3 + tc * 512:RS3 + tc * 512 + 512],
                                 start=False, stop=True)
            return r
        p.op("ten", daug, reads=["rsfrow", "c_nr", "psD"], writes=["psD"])
        for j in range(no):
            for tc in range(2):
                pmv = [ps_a[:, 0:512], ps_a[:, 512:1024],
                       ps_b[:, 0:512], ps_b[:, 512:1024],
                       ps_c[:, 0:512], ps_d[:, 0:512]][j * 2 + tc]
                p.op("sca", lambda s, pmv=pmv, oc=oc0 + j, tc=tc: s.copy(
                    bg(DFM + oc * 1024 + tc * 512, 512), pmv),
                    reads=["psD"], writes=["dfm"])

    p.barrier()
    if "dbg_dfm" in g:
        p.dma("gps", g["dbg_dfm"], bg(DFM, 16384), reads=["dfm"], is_output=True)

    if KLIM < 9:
        return
    for i in range(OB):
        for dc in range(DC):
            pt = (pb0 if dc % 2 == 0 else pb1)[:, 0:128]
            kp = f"pb{dc % 2}"
            p.op("ten", lambda t, dc=dc, i=i, pt=pt: t.transpose(
                pt, bg(DFM + dc * 1024 + i * 128, 128), c_id[:]),
                reads=["dfm", "c_id"], writes=[kp])
            p.op("sca", lambda s, pt=pt, dc=dc: s.copy(
                bg(0 + dc * 128, 128), pt),
                reads=[kp], writes=["dtm"])
        dtm = bg(0, 2048)
        mur, rsr, kmur, _ = ln_stats(dtm, D, SA, "dtm", pl(PXB, 2048), "ost")
        var = st(SA + 1)
        gcol = st(GCL + i)
        a1, a2, a3 = st(LN4), st(LN4 + 1), st(LN4 + 2)
        p.op("vec", lambda v, var=var, a1=a1: v.tensor_scalar(
            a1, var, -1e-5, None, ALU.add), reads=[f"st{SA}_ex"], writes=["l4v"])
        p.op("vec", lambda v, gcol=gcol, a2=a2: v.tensor_tensor(
            a2, gcol, gcol, ALU.mult), reads=[f"gc_{i}"], writes=["l4g2"])
        p.op("vec", lambda v, a1=a1, a2=a2: v.tensor_tensor(
            a1, a1, a2, ALU.mult), reads=["l4v", "l4g2"], writes=["l4v"])
        p.op("vec", lambda v, a1=a1: v.tensor_scalar(
            a1, a1, 1e-5, None, ALU.add), reads=["l4v"], writes=["l4v"])
        p.op("sca", lambda s, a1=a1, a3=a3: s.activation(a3, a1, AF.Sqrt),
             reads=["l4v"], writes=["l4s"])
        rstd = st(LN4 + 3)
        p.op("vec", lambda v, a3=a3, rstd=rstd: v.reciprocal(rstd, a3),
             reads=["l4s"], writes=["l4r"])
        p.op("vec", lambda v, rstd=rstd, a3=a3: v.tensor_tensor(
            a3, rstd, rstd, ALU.mult), reads=["l4r"], writes=["l4s"])
        p.op("vec", lambda v, a3=a3, a1=a1: v.tensor_tensor(
            a3, a3, a1, ALU.mult), reads=["l4s", "l4v"], writes=["l4s"])
        p.op("vec", lambda v, a3=a3: v.tensor_scalar(
            a3, a3, -0.5, 1.5, ALU.mult, ALU.add), reads=["l4s"], writes=["l4s"])
        p.op("vec", lambda v, rstd=rstd, a3=a3: v.tensor_tensor(
            rstd, rstd, a3, ALU.mult), reads=["l4r", "l4s"], writes=["l4r"])
        sA_, sB_ = st(LN4 + 4), st(LN4 + 5)
        p.op("vec", lambda v, gcol=gcol, rstd=rstd, sA_=sA_: v.tensor_tensor(
            sA_, gcol, rstd, ALU.mult), reads=[f"gc_{i}", "l4r"], writes=["l4a"])
        p.op("vec", lambda v, sA_=sA_, mur=mur, sB_=sB_: v.tensor_tensor(
            sB_, sA_, mur, ALU.mult), reads=["l4a", kmur], writes=["l4b"])
        p.op("vec", lambda v, sB_=sB_: v.tensor_scalar(
            sB_, sB_, -1.0, None, ALU.mult), reads=["l4b"], writes=["l4b"])
        ost = pl(PXB, 2048)
        p.op("vec", lambda v, dtm=dtm, sA_=sA_, sB_=sB_, ost=ost: v.tensor_scalar(
            ost, dtm, sA_, sB_, ALU.mult, ALU.add),
            reads=["dtm", "l4a", "l4b"], writes=["ost"])
        p.dma("syn", tq[:, 0:2048], x1r[i], reads=[f"x1p{i}"], writes=["x1t"])
        p.op("vec", lambda v, ost=ost: v.tensor_tensor(
            ost, ost, tq[:, 0:2048], ALU.add), reads=["ost", "x1t"], writes=["ost"])
        p.dma("syn", outr[i], ost, reads=["ost"], writes=["outd"], is_output=True)



def kernel(**inputs):
    sh, cores = host_prep(inputs)
    nc = build_nc()
    names = ["wqT", "wkT", "wvT", "woT", "wgS", "wuS", "wdS", "ident",
             "ident32", "onesf", "h128", "p2"]
    in_maps = []
    for core in range(8):
        d = {k: sh[k] for k in names}
        for k in ("x_seq", "c_q", "s_q", "c_kp", "s_kp", "maskT"):
            d[k] = cores[core][k]
        in_maps.append(d)
    res = run_bass_kernel_spmd(nc, in_maps, core_ids=list(range(8)))
    outp = np.zeros((B, T, D), dtype=np.float32)
    for core in range(8):
        outp[core // 2, cores[core]["own_tok"]] = np.asarray(res.results[core]["out"])
    return outp
